# version 10
# baseline (speedup 1.0000x reference)
import numpy as np
import ml_dtypes
import concourse.bass as bass
import concourse.mybir as mybir
from concourse.bass_utils import run_bass_kernel_spmd

F32 = mybir.dt.float32
BF16 = mybir.dt.bfloat16
AF = mybir.ActivationFunctionType
ALU = mybir.AluOpType
AXX = mybir.AxisListType.X

B, T, D = 8, 2048, 1024
NH, HD, DR, LAT = 16, 64, 16, 256
NT = T // 128
NCH = D // 128
NKT = T // 128
NQB = 2
QBS = 1024
NPAIR = NH // 2
EPS = 1e-5

_WSPECS = [
    ("wqT", D, D), ("wkaT", D, LAT), ("wkbT", LAT, D),
    ("wvaT", D, LAT), ("wvbT", LAT, D), ("woT", D, D),
    ("wrT", D, D), ("wkT2", D, D), ("wvT2", D, D),
]


def _rope_tables():
    i64 = np.arange(0, HD, 2, dtype=np.float64)[: DR // 2]
    inv = 1.0 / 10000.0 ** (i64 / HD)
    ang = np.arange(NH, dtype=np.float64)[:, None] * inv[None, :]
    cos = np.cos(ang).astype(np.float32).reshape(NH * 8)
    sin = np.sin(ang).astype(np.float32).reshape(NH * 8)
    cos_t = np.ascontiguousarray(np.broadcast_to(cos[None, :], (128, NH * 8)))
    sin_t = np.ascontiguousarray(np.broadcast_to(sin[None, :], (128, NH * 8)))
    return cos_t, sin_t


def build_nc():
    nc = bass.Bass("TRN2", target_bir_lowering=False, debug=False,
                   detect_race_conditions=False)

    x_in = nc.dram_tensor("x_in", [T, D], F32, kind="ExternalInput").ap()
    w_in = {n: nc.dram_tensor(n, [r, c], F32, kind="ExternalInput").ap()
            for (n, r, c) in _WSPECS}
    cos_in = nc.dram_tensor("cos_in", [128, NT * 8], F32, kind="ExternalInput").ap()
    sin_in = nc.dram_tensor("sin_in", [128, NT * 8], F32, kind="ExternalInput").ap()
    mur_in = nc.dram_tensor("mur_in", [128, NCH], F32, kind="ExternalInput").ap()
    muk_in = nc.dram_tensor("muk_in", [128, NCH], F32, kind="ExternalInput").ap()
    idb_in = nc.dram_tensor("idb_in", [128, 128], BF16, kind="ExternalInput").ap()
    idf_in = nc.dram_tensor("idf_in", [128, 128], F32, kind="ExternalInput").ap()
    y_out = nc.dram_tensor("y_out", [T, D], F32, kind="ExternalOutput").ap()
    w_bf = {n: nc.dram_tensor(n + "_bf", [r, c], BF16, kind="Internal").ap()
            for (n, r, c) in _WSPECS}
    x1_sp = nc.dram_tensor("x1_sp", [T, D], F32, kind="Internal").ap()
    canary = nc.dram_tensor("canary", [1, 32], F32, kind="Internal").ap()

    from contextlib import ExitStack
    stack = ExitStack()

    def sb(name, shape, dt):
        return stack.enter_context(nc.sbuf_tensor(name, shape, dt))

    G1 = sb("G1", [128, 16384], BF16)
    G2 = sb("G2", [128, 16640], BF16)
    G3 = sb("G3", [128, 16384], BF16)
    G5 = sb("G5", [128, 16384], BF16)
    G6 = sb("G6", [128, 8192], BF16)
    G7 = sb("G7", [128, 2048], BF16)
    F1 = sb("F1", [128, 2048], F32)
    F2 = sb("F2", [128, 2048], F32)
    FK = sb("FK", [128, 512], F32)
    CS = sb("CSt", [128, 2 * NT * 8], F32)
    MU = sb("MUt", [128, 2 * NCH], F32)
    IDB = sb("IDB", [128, 128], BF16)
    IDF = sb("IDF", [128, 128], F32)
    SSUM = sb("SSUM", [128, 2], F32)
    SSQ = sb("SSQ", [128, 2], F32)
    MEAN = sb("MEAN", [128, 2], F32)
    VAR = sb("VAR", [128, 2], F32)
    RSTD = sb("RSTD", [128, 2], F32)
    RTMP = sb("RTMP", [128, 256], F32)
    NQT = sb("NQT", [128, 16], F32)
    NQA = sb("NQA", [128, 16], F32)
    NKA = sb("NKA", [128, 16], F32)
    CROW = sb("CROW", [1, 32], F32)
    NEGC = sb("NEGC", [128, 1], F32)
    PP = sb("PP", [128, 2048], BF16)
    RDBn = sb("RDBn", [128, 1024], F32)
    DENR = sb("DENR", [128, 512], F32)
    DENC = sb("DENC", [128, 512], F32)
    ONESB = sb("ONESB", [128, 1], BF16)
    WSF = sb("WSF", [128, 1024], F32)
    WSB = sb("WSB", [128, 1024], BF16)

    sx = stack.enter_context(nc.semaphore("sx"))
    sg = stack.enter_context(nc.semaphore("sg"))
    st_ = stack.enter_context(nc.semaphore("st"))
    sv = stack.enter_context(nc.semaphore("sv"))
    ss = stack.enter_context(nc.semaphore("ss"))
    scw = stack.enter_context(nc.semaphore("scw"))
    sldA = stack.enter_context(nc.semaphore("sldA"))
    sldB = stack.enter_context(nc.semaphore("sldB"))
    SEM = dict(x=sx, g=sg, t=st_, v=sv, s=ss)
    C = dict(x=0, g=0, t=0, v=0, s=0)

    def inc(key, instr, k=1):
        instr.then_inc(SEM[key], k)
        C[key] += k
        return C[key]

    CANARY_N = [0]

    def emit_canary(waits=()):
        idx = CANARY_N[0]
        CANARY_N[0] += 1

        @blk.gpsimd
        def _(gpsimd):
            for (sem, val) in waits:
                gpsimd.wait_ge(sem, val)
            inc("g", gpsimd.dma_start(out=canary[0:1, idx:idx + 1],
                                      in_=MU[0:1, 0:1]), 16)

    blk = stack.enter_context(nc.Block())

    HTv = G1[:].rearrange("p (c t) -> p c t", c=NCH)
    Qv = G1[:].rearrange("p (c t) -> p c t", c=NCH)
    QTv = G2[:, 0:16384].rearrange("p (i n) -> p i n", i=NT)
    KTv = QTv
    VVv = G2[:, 0:16384].rearrange("p (i n) -> p i n", i=NT)
    H2Tv = G2[:, 0:16384].rearrange("p (c t) -> p c t", c=NCH)
    Kv = G3[:].rearrange("p (c t) -> p c t", c=NCH)
    latKv = G6[:, 0:4096].rearrange("p (l t) -> p l t", l=2)
    latVv = G6[:, 4096:8192].rearrange("p (l t) -> p l t", l=2)
    woTv = G6[:].rearrange("p (c n) -> p c n", c=NCH)
    wqf = G5[:, 0:8192].rearrange("p (c n) -> p c n", c=NCH)
    wkas = G5[:, 8192:10240].rearrange("p (c n) -> p c n", c=NCH)
    wkbs = G5[:, 10240:12288].rearrange("p (l n) -> p l n", l=2)
    wvas = G5[:, 12288:14336].rearrange("p (c n) -> p c n", c=NCH)
    wvbs = G5[:, 14336:16384].rearrange("p (l n) -> p l n", l=2)
    Oallv = G5[:].rearrange("p (c n) -> p c n", c=NCH)
    XRv = G5[:, 0:4096].rearrange("p (c n) -> p c n", c=NCH)
    XKv = G5[:, 4096:8192].rearrange("p (c n) -> p c n", c=NCH)
    Rv = G5[:, 8192:12288].rearrange("p (i n) -> p i n", i=4)
    WrTv = G1[:, 0:8192].rearrange("p (c n) -> p c n", c=NCH)
    kactv = G1[:, 8192:12288].rearrange("p (c n) -> p c n", c=NCH)
    WkTv = G3[:, 0:8192].rearrange("p (c n) -> p c n", c=NCH)
    WvTv = G3[:, 8192:16384].rearrange("p (c n) -> p c n", c=NCH)
    cosv = CS[:, 0:NT * 8].rearrange("p (i f) -> p i f", i=NT)
    sinv = CS[:, NT * 8:].rearrange("p (i f) -> p i f", i=NT)
    G7v = G7[:].rearrange("p (k n) -> p k n", k=2)

    @blk.sync
    def _(sync):
        inc("x", sync.dma_start(out=IDB[:], in_=idb_in), 16)
        inc("x", sync.dma_start(out=IDF[:], in_=idf_in), 16)
        inc("x", sync.dma_start(out=CS[:, 0:NT * 8], in_=cos_in), 16)
        inc("x", sync.dma_start(out=CS[:, NT * 8:], in_=sin_in), 16)
        inc("x", sync.dma_start(out=MU[:, 0:NCH], in_=mur_in), 16)
        inc("x", sync.dma_start(out=MU[:, NCH:], in_=muk_in), 16)

    _wdims = {n: (r, c) for (n, r, c) in _WSPECS}

    def _chunks_of(names):
        out = []
        for n in names:
            r, c = _wdims[n]
            for p in range(r // 128):
                for h in range(0, c, 512):
                    out.append((n, p, h, min(512, c - h)))
        return out

    chunks = (_chunks_of(["wkaT", "wvaT", "wkbT", "wvbT", "wqT"])
              + _chunks_of(["woT", "wrT", "wkT2", "wvT2"]))
    NCK = len(chunks)
    N_WEAVE = NCK
    SCW_SMALLW = 24
    SCW_WQ = 40

    def _direct_dst(n, p, h, w):
        if n == "wkaT":
            return wkas[:, p, 0:w]
        if n == "wvaT":
            return wvas[:, p, 0:w]
        if n == "wkbT":
            return wkbs[:, p, h:h + w]
        if n == "wvbT":
            return wvbs[:, p, h:h + w]
        if n == "wqT":
            return wqf[:, p, h:h + w]
        return None

    bounce_idx = {}
    for j, (n, p, h, w) in enumerate(chunks):
        if n in ("woT", "wrT", "wkT2", "wvT2"):
            bounce_idx[j] = len(bounce_idx)
    g_store, P0g = {}, {}

    @blk.gpsimd
    def _(gpsimd):
        for j, (n, p, h, w) in enumerate(chunks):
            if j not in bounce_idx:
                continue
            bj = bounce_idx[j]
            gpsimd.wait_ge(scw, j + 1)
            if bj >= 1:
                gpsimd.wait_ge(sg, g_store[bj - 1])
            s2 = (bj % 2) * 512
            inc("g", gpsimd.dma_start(
                out=w_bf[n][p * 128:(p + 1) * 128, h:h + w],
                in_=WSB[:, s2:s2 + w]), 16)
            g_store[bj] = C["g"]
            if j + 1 == NCK or chunks[j + 1][0] != n:
                P0g[n] = C["g"]
    P0_done_g = C["g"]

    CAST_J = [0]
    LD_J = [0]

    def emit_casts(scalar, budget, limit=N_WEAVE):
        while CAST_J[0] < limit and budget > 0:
            j = CAST_J[0]
            while LD_J[0] < min(j + 2, limit):
                i = LD_J[0]
                n, p, h, w = chunks[i]
                if i >= 2:
                    scalar.wait_ge(scw, i - 1)
                s = (i % 2) * 512
                scalar.dma_start(out=WSF[:, s:s + w],
                                 in_=w_in[n][p * 128:(p + 1) * 128, h:h + w]
                                 ).then_inc(sldA if i % 2 == 0 else sldB, 16)
                LD_J[0] += 1
            n, p, h, w = chunks[j]
            scalar.wait_ge(sldA if j % 2 == 0 else sldB, 16 * (j // 2 + 1))
            dst = _direct_dst(n, p, h, w)
            if dst is None:
                bj = bounce_idx[j]
                if bj >= 2:
                    scalar.wait_ge(sg, g_store[bj - 2])
                dst = WSB[:, (bj % 2) * 512:(bj % 2) * 512 + w]
            s = (j % 2) * 512
            scalar.activation(dst, WSF[:, s:s + w],
                              AF.Copy).then_inc(scw, 1)
            CAST_J[0] += 1
            budget -= 1
    emit_canary([(sg, P0_done_g), (sx, 16 * 6)])

    def emit_ln_phase(src_hbm, dstv, ps_t, sync_waits, vec_waits,
                      scalar_extra=None):
        X0, V0, S0, T0 = C["x"], C["v"], C["s"], C["t"]

        @blk.sync
        def _(sync):
            for (sem, val) in sync_waits:
                sync.wait_ge(sem, val)
            for i in range(NT):
                if i >= 2:
                    sync.wait_ge(sv, V0 + 9 * (i - 2) + 8)
                s = i % 2
                inc("x", sync.dma_start(out=F1[:, s * 1024:(s + 1) * 1024],
                                        in_=src_hbm[i * 128:(i + 1) * 128, :]), 16)

        @blk.scalar
        def _(scalar):
            for i in range(NT):
                s = i % 2
                if scalar_extra:
                    scalar_extra(scalar, i)
                scalar.wait_ge(sx, X0 + 16 * (i + 1))
                if i >= 2:
                    scalar.wait_ge(sv, V0 + 9 * (i - 2) + 1)
                inc("s", scalar.activation(F2[:, s * 1024:(s + 1) * 1024],
                                           F1[:, s * 1024:(s + 1) * 1024], AF.Square))
                scalar.wait_ge(sv, V0 + 9 * i + 7)
                if i >= 2:
                    scalar.wait_ge(sv, V0 + 9 * (i - 2) + 8)
                inc("s", scalar.activation(RSTD[:, s:s + 1], VAR[:, s:s + 1], AF.Sqrt))

        @blk.vector
        def _(vector):
            for (sem, val) in vec_waits:
                vector.wait_ge(sem, val)
            for i in range(NT):
                s = i % 2
                vector.wait_ge(sx, X0 + 16 * (i + 1))
                vector.reduce_sum(SSUM[:, s:s + 1], F1[:, s * 1024:(s + 1) * 1024], axis=AXX)
                vector.wait_ge(ss, S0 + 2 * i + 1)
                inc("v", vector.reduce_sum(SSQ[:, s:s + 1], F2[:, s * 1024:(s + 1) * 1024], axis=AXX))
                vector.wait_ge(sv, V0 + 9 * i + 1)
                inc("v", vector.tensor_scalar(MEAN[:, s:s + 1], SSUM[:, s:s + 1], 1.0 / D, None, ALU.mult))
                inc("v", vector.tensor_scalar(VAR[:, s:s + 1], SSQ[:, s:s + 1], 1.0 / D, None, ALU.mult))
                vector.wait_ge(sv, V0 + 9 * i + 3)
                inc("v", vector.tensor_tensor(SSQ[:, s:s + 1], MEAN[:, s:s + 1], MEAN[:, s:s + 1], ALU.mult))
                vector.wait_ge(sv, V0 + 9 * i + 4)
                inc("v", vector.tensor_tensor(VAR[:, s:s + 1], VAR[:, s:s + 1], SSQ[:, s:s + 1], ALU.subtract))
                vector.wait_ge(sv, V0 + 9 * i + 5)
                inc("v", vector.tensor_scalar(VAR[:, s:s + 1], VAR[:, s:s + 1], EPS, None, ALU.add))
                vector.wait_ge(sv, V0 + 9 * i + 6)
                inc("v", vector.reciprocal(VAR[:, s:s + 1], VAR[:, s:s + 1]))
                vector.wait_ge(ss, S0 + 2 * i + 2)
                inc("v", vector.tensor_scalar(G7v[:, s, :], F1[:, s * 1024:(s + 1) * 1024],
                                              MEAN[:, s:s + 1], RSTD[:, s:s + 1],
                                              ALU.subtract, ALU.mult))
                vector.wait_ge(st_, T0 + i + 1)
                inc("v", vector.tensor_copy(
                    dstv[:, :, i * 128:(i + 1) * 128],
                    ps_t[:, s, :].rearrange("p (c n) -> p c n", c=NCH)))

        @blk.tensor
        def _(tensor):
            for i in range(NT):
                s = i % 2
                tensor.wait_ge(sv, V0 + 9 * i + 8)
                if i >= 2:
                    tensor.wait_ge(sv, V0 + 9 * (i - 2) + 9)
                last = None
                for c in range(NCH):
                    last = tensor.transpose(
                        ps_t[:, s, c * 128:(c + 1) * 128],
                        G7v[:, s, c * 128:(c + 1) * 128], IDB[:])
                inc("t", last)
        assert C["v"] == V0 + 9 * NT and C["s"] == S0 + 2 * NT and C["t"] == T0 + NT

    ps_a_ctx = nc.psum_tensor("ps_a", [128, 1024], F32)
    ps_b_ctx = nc.psum_tensor("ps_b", [128, 1024], F32)
    ps_a = ps_a_ctx.__enter__()
    ps_b = ps_b_ctx.__enter__()
    ps_t_ctx = nc.psum_tensor("ps_t", [128, 2048], BF16)
    ps_t = ps_t_ctx.__enter__()
    ps_tv = ps_t[:].rearrange("p (k n) -> p k n", k=2)

    emit_ln_phase(x_in, HTv, ps_tv, sync_waits=[], vec_waits=[],
                  scalar_extra=lambda scalar, i: emit_casts(scalar, 3))
    emit_canary([(sv, C["v"])])

    T0, S0 = C["t"], C["s"]
    fills = [(w, lc, tb) for w in range(2) for lc in range(2) for tb in range(4)]
    HT_ready_v = C["v"]

    @blk.tensor
    def _(tensor):
        tensor.wait_ge(scw, SCW_SMALLW)
        tensor.wait_ge(sv, HT_ready_v)
        for j, (which, lc, tb) in enumerate(fills):
            pss = (ps_a if j % 2 == 0 else ps_b)[:, 0:512]
            if j >= 2:
                tensor.wait_ge(ss, S0 + j - 1)
            wsl = wkas if which == 0 else wvas
            last = None
            for c in range(NCH):
                last = tensor.matmul(pss, wsl[:, c, lc * 128:(lc + 1) * 128],
                                     HTv[:, c, tb * 512:(tb + 1) * 512],
                                     start=(c == 0), stop=(c == NCH - 1))
            inc("t", last)

    @blk.scalar
    def _(scalar):
        for j, (which, lc, tb) in enumerate(fills):
            pss = (ps_a if j % 2 == 0 else ps_b)[:, 0:512]
            emit_casts(scalar, 1)
            scalar.wait_ge(st_, T0 + j + 1)
            dst = (latKv if which == 0 else latVv)[:, lc, tb * 512:(tb + 1) * 512]
            inc("s", scalar.activation(dst, pss, AF.Copy))
    assert C["t"] == T0 + 16 and C["s"] == S0 + 16
    emit_canary([(ss, C["s"])])

    def emit_qkt(src_is_q):
        T0b, S0b, V0b = C["t"], C["s"], C["v"]
        nacc = NCH if src_is_q else 2

        @blk.tensor
        def _(tensor):
            tensor.wait_ge(scw, SCW_WQ if src_is_q else SCW_SMALLW)
            for half in range(2):
                for i in range(NT):
                    j = half * NT + i
                    pss = (ps_a if j % 2 == 0 else ps_b)[:, 0:512]
                    if j >= 2:
                        tensor.wait_ge(ss, S0b + (j - 2) + 1)
                        tensor.wait_ge(sv, V0b + 6 * (j - 2) + 6)
                    else:
                        tensor.wait_ge(ss, S0b)
                        tensor.wait_ge(sv, V0b)
                    last = None
                    for a in range(nacc):
                        if src_is_q:
                            lhs = HTv[:, a, i * 128:(i + 1) * 128]
                            rhs = wqf[:, a, half * 512:(half + 1) * 512]
                        else:
                            lhs = latKv[:, a, i * 128:(i + 1) * 128]
                            rhs = wkbs[:, a, half * 512:(half + 1) * 512]
                        last = tensor.matmul(pss, lhs, rhs,
                                             start=(a == 0), stop=(a == nacc - 1))
                    inc("t", last)

        @blk.scalar
        def _(scalar):
            for half in range(2):
                for i in range(NT):
                    j = half * NT + i
                    pss = (ps_a if j % 2 == 0 else ps_b)[:, 0:512]
                    emit_casts(scalar, 1)
                    scalar.wait_ge(st_, T0b + j + 1)
                    inc("s", scalar.activation(
                        QTv[:, i, half * 512:(half + 1) * 512], pss, AF.Copy))

        @blk.vector
        def _(vector):
            for half in range(2):
                for i in range(NT):
                    j = half * NT + i
                    ps_full = ps_a if j % 2 == 0 else ps_b
                    vector.wait_ge(ss, S0b + j + 1)
                    ap = ps_full[:, 0:512].rearrange("p (h a b) -> p h a b", h=8, b=2)
                    xa = ap[:, :, 0:8, 0:1]
                    xb = ap[:, :, 0:8, 1:2]
                    cv = cosv[:, half * 8:(half + 1) * 8, :].unsqueeze(3)
                    sn = sinv[:, half * 8:(half + 1) * 8, :].unsqueeze(3)
                    ta = RTMP[:, 0:64].rearrange("p (h a b) -> p h a b", h=8, a=8)
                    tb = RTMP[:, 64:128].rearrange("p (h a b) -> p h a b", h=8, a=8)
                    tg = RTMP[:, 128:192].rearrange("p (h a b) -> p h a b", h=8, a=8)
                    td = RTMP[:, 192:256].rearrange("p (h a b) -> p h a b", h=8, a=8)
                    out = QTv[:, i, half * 512:(half + 1) * 512] \
                        .rearrange("p (h a b) -> p h a b", h=8, b=2)
                    oe = out[:, :, 0:8, 0:1]
                    oo = out[:, :, 0:8, 1:2]
                    vector.tensor_tensor(ta, xa, cv, ALU.mult)
                    vector.tensor_tensor(tb, xb, sn, ALU.mult)
                    vector.tensor_tensor(tg, xa, sn, ALU.mult)
                    inc("v", vector.tensor_tensor(td, xb, cv, ALU.mult))
                    vector.wait_ge(sv, C["v"])
                    vector.tensor_tensor(oe, ta, tb, ALU.subtract)
                    inc("v", vector.tensor_tensor(oo, tg, td, ALU.add), 5)
        assert C["t"] == T0b + 2 * NT and C["s"] == S0b + 2 * NT
        assert C["v"] == V0b + 6 * 2 * NT
        return V0b

    def emit_norms(acc, rope_v0):
        S0n, V0n = C["s"], C["v"]

        @blk.scalar
        def _(scalar):
            for i in range(NT):
                s = i % 2
                scalar.wait_ge(sv, rope_v0 + 6 * (NT + i) + 6)
                if i >= 2:
                    scalar.wait_ge(sv, V0n + 2 * (i - 2) + 1)
                inc("s", scalar.activation(F2[:, s * 1024:(s + 1) * 1024],
                                           QTv[:, i, :], AF.Square))

        @blk.vector
        def _(vector):
            for i in range(NT):
                s = i % 2
                vector.wait_ge(ss, S0n + i + 1)
                inc("v", vector.reduce_sum(
                    NQT[:].rearrange("p (h u) -> p h u", h=16, u=1),
                    F2[:, s * 1024:(s + 1) * 1024].rearrange("p (h u) -> p h u", h=16),
                    axis=AXX))
                vector.wait_ge(sv, C["v"])
                if i == 0:
                    inc("v", vector.tensor_copy(acc[:], NQT[:]))
                else:
                    inc("v", vector.tensor_tensor(acc[:], acc[:], NQT[:], ALU.max))

    def emit_big_transpose(srcv, dstv_, ready_v):
        T0c, V0c = C["t"], C["v"]

        @blk.tensor
        def _(tensor):
            tensor.wait_ge(sv, ready_v)
            for i in range(NT):
                s = i % 2
                if i >= 2:
                    tensor.wait_ge(sv, V0c + (i - 2) + 1)
                last = None
                for c in range(NCH):
                    last = tensor.transpose(ps_tv[:, s, c * 128:(c + 1) * 128],
                                            srcv[:, i, c * 128:(c + 1) * 128], IDB[:])
                inc("t", last)

        @blk.vector
        def _(vector):
            for i in range(NT):
                s = i % 2
                vector.wait_ge(st_, T0c + i + 1)
                inc("v", vector.tensor_copy(
                    dstv_[:, :, i * 128:(i + 1) * 128],
                    ps_tv[:, s, :].rearrange("p (c n) -> p c n", c=NCH)))
        assert C["t"] == T0c + NT and C["v"] == V0c + NT

    q_rope_v0 = emit_qkt(True)
    emit_canary([(sv, C["v"]), (ss, C["s"])])
    emit_norms(NQA, q_rope_v0)
    emit_big_transpose(QTv, Qv, C["v"])
    k_rope_v0 = emit_qkt(False)
    emit_canary([(sv, C["v"]), (ss, C["s"])])
    emit_norms(NKA, k_rope_v0)
    emit_big_transpose(KTv, Kv, C["v"])
    K_done_v = C["v"]

    T0v, V0v, S0v = C["t"], C["v"], C["s"]

    @blk.tensor
    def _(tensor):
        tensor.wait_ge(sv, K_done_v)
        tensor.wait_ge(ss, S0v)
        tensor.wait_ge(scw, SCW_SMALLW)
        for i in range(NT):
            for half in range(2):
                j = i * 2 + half
                pss = (ps_a if j % 2 == 0 else ps_b)[:, 0:512]
                if j >= 2:
                    tensor.wait_ge(sv, V0v + 1 + (j - 1))
                last = None
                for a in range(2):
                    last = tensor.matmul(pss, latVv[:, a, i * 128:(i + 1) * 128],
                                         wvbs[:, a, half * 512:(half + 1) * 512],
                                         start=(a == 0), stop=(a == 1))
                inc("t", last)

    @blk.vector
    def _(vector):
        vector.wait_ge(sv, K_done_v)
        inc("v", vector.memset(ONESB[:], 1.0))
        for i in range(NT):
            for half in range(2):
                j = i * 2 + half
                pss = (ps_a if j % 2 == 0 else ps_b)[:, 0:512]
                vector.wait_ge(st_, T0v + j + 1)
                inc("v", vector.tensor_copy(
                    VVv[:, i, half * 512:(half + 1) * 512], pss))
    VV_done_v = C["v"]
    VV_fill_done_t = C["t"]
    assert C["t"] == T0v + 32 and C["v"] == V0v + 33
    emit_canary([(sv, C["v"])])

    T0n, V0n, S0n = C["t"], C["v"], C["s"]

    @blk.tensor
    def _(tensor):
        tensor.wait_ge(sv, VV_done_v)
        inc("t", tensor.transpose(ps_a[0:16, 0:128], NQA[:], IDF[:]))
        inc("t", tensor.transpose(ps_b[0:16, 0:128], NKA[:], IDF[:]))

    @blk.vector
    def _(vector):
        vector.wait_ge(st_, T0n + 2)
        vector.reduce_max(NQT[0:16, 0:1], ps_a[0:16, 0:128], axis=AXX)
        inc("v", vector.reduce_max(NQT[0:16, 1:2], ps_b[0:16, 0:128], axis=AXX))
        vector.wait_ge(sv, C["v"])
        inc("v", vector.tensor_tensor(NQT[0:16, 2:3], NQT[0:16, 0:1],
                                      NQT[0:16, 1:2], ALU.mult))

    @blk.scalar
    def _(scalar):
        scalar.wait_ge(sv, V0n + 2)
        inc("s", scalar.activation(NQT[0:16, 3:4], NQT[0:16, 2:3], AF.Sqrt))

    @blk.vector
    def _(vector):
        vector.wait_ge(ss, S0n + 1)
        inc("v", vector.tensor_scalar(NQT[0:16, 4:5], NQT[0:16, 3:4],
                                      0.125, None, ALU.mult))

    @blk.tensor
    def _(tensor):
        tensor.wait_ge(sv, V0n + 3)
        inc("t", tensor.transpose(ps_a[0:1, 0:16], NQT[0:16, 4:5], IDF[0:16, 0:16]))

    @blk.vector
    def _(vector):
        vector.wait_ge(st_, T0n + 3)
        inc("v", vector.tensor_copy(CROW[0:1, 0:16], ps_a[0:1, 0:16]))
        vector.wait_ge(sv, C["v"])
        inc("v", vector.reduce_max(CROW[0:1, 16:17], CROW[0:1, 0:16], axis=AXX))
        vector.wait_ge(sv, C["v"])
        inc("v", vector.tensor_scalar(CROW[0:1, 17:18], CROW[0:1, 16:17],
                                      -60.0, 0.0, ALU.add, ALU.max))
        vector.wait_ge(sv, C["v"])
        inc("v", vector.tensor_scalar(CROW[0:1, 18:19], CROW[0:1, 17:18],
                                      -1.0, None, ALU.mult))

    @blk.gpsimd
    def _(gpsimd):
        gpsimd.wait_ge(sv, V0n + 7)
        inc("g", gpsimd.dma_start(
            out=NEGC[:], in_=CROW[0:1, 18:19].unsqueeze(1)
            .broadcast_to([1, 128, 1])), 16)
    negc_g = C["g"]
    emit_canary([(sg, negc_g)])

    @blk.scalar
    def _(scalar):
        emit_casts(scalar, NCK)
    assert CAST_J[0] == NCK

    ps_t_ctx.__exit__(None, None, None)
    ps_b_ctx.__exit__(None, None, None)
    ps_a_ctx.__exit__(None, None, None)

    psS_ctx = nc.psum_tensor("psS", [128, 2048], F32)
    psAV_ctx = nc.psum_tensor("psAV", [128, 1024], F32)
    psDN_ctx = nc.psum_tensor("psDN", [128, 512], F32)
    psWO_ctx = nc.psum_tensor("psWO", [128, 512], F32)
    psS = psS_ctx.__enter__()
    psAV = psAV_ctx.__enter__()
    psDN = psDN_ctx.__enter__()
    psWO = psWO_ctx.__enter__()

    NU = NQB * NPAIR * 32

    def uidx(u):
        G, s = divmod(u, 32)
        kt, o = divmod(s, 2)
        return G // NPAIR, G % NPAIR, kt, o

    wo_groups = [(qb, tt, o) for qb in range(NQB) for tt in range(8)
                 for o in range(2)]
    NKTI = NU // 2
    wo_before = {}
    for k in range(16):
        wo_before.setdefault(132 + 4 * k, []).append(k)
    wo_tail = list(range(16, 32))

    def wo_bank(k):
        if k < 16:
            return psWO
        return psWO if (k - 16) % 2 == 0 else psDN

    events = []
    for it in range(NKTI + 1):
        for k in wo_before.get(it, []):
            events.append(("wo", k))
        if it < NKTI:
            events.append(("S", 2 * it))
            events.append(("S", 2 * it + 1))
        if it >= 1:
            events.append(("dnav", 2 * it - 2))
            events.append(("dnav", 2 * it - 1))
    for k in wo_tail:
        events.append(("wo", k))

    t_S, t_dnav, t_wo = {}, {}, {}
    tc = C["t"]
    for ev in events:
        tc += 1
        if ev[0] == "S":
            t_S[ev[1]] = tc
        elif ev[0] == "dnav":
            t_dnav[ev[1]] = tc
        else:
            t_wo[ev[1]] = tc

    S0p = C["s"]
    s_exp = {u: S0p + u + 1 for u in range(NU)}

    vec_events = []
    evac_in_G = {G: [] for G in range(2 * NPAIR)}
    for k in range(16):
        evac_in_G[min((132 + 4 * k) // 16, 2 * NPAIR - 1)].append(k)
    for G in range(2 * NPAIR):
        for k in evac_in_G[G]:
            vec_events.append(("x1", k))
        vec_events.append(("evac", G))
        vec_events.append(("dncp", G))
        vec_events.append(("recip", G))
        vec_events.append(("norm", G))
    for k in wo_tail:
        vec_events.append(("x1", k))

    v_evac, v_dncp, v_recip, v_norm, v_x1 = {}, {}, {}, {}, {}
    vc = C["v"]
    for ev in vec_events:
        vc += 1
        if ev[0] == "evac":
            v_evac[ev[1]] = vc
        elif ev[0] == "dncp":
            v_dncp[ev[1]] = vc
        elif ev[0] == "recip":
            v_recip[ev[1]] = vc
        elif ev[0] == "norm":
            v_norm[ev[1]] = vc
        else:
            v_x1[ev[1]] = vc

    g_rdb = {}
    gc = C["g"]
    for G in range(2 * NPAIR):
        gc += 64
        g_rdb[G] = gc

    x_wot = C["x"] + 16
    x_rw, x_st = {}, {}
    xc = x_wot
    for qb in range(NQB):
        for tt in range(8):
            xc += 16
            x_rw[(qb, tt)] = xc
            if tt >= 1:
                xc += 16
                x_st[(qb, tt - 1)] = xc
        xc += 16
        x_st[(qb, 7)] = xc

    @blk.tensor
    def _(tensor):
        for ev in events:
            if ev[0] == "S":
                u = ev[1]
                qb, p, kt, o = uidx(u)
                if u >= 2:
                    tensor.wait_ge(ss, s_exp[u - 2])
                else:
                    tensor.wait_ge(sv, VV_done_v)
                    tensor.wait_ge(sg, negc_g)
                slot = (u % 2) * 1024
                q0 = qb * QBS + o * 512
                tensor.matmul(psS[:, slot:slot + 512],
                              Kv[0:64, p, kt * 128:(kt + 1) * 128],
                              Qv[0:64, p, q0:q0 + 512],
                              start=True, stop=True, tile_position=(0, 0))
                last = tensor.matmul(psS[:, slot + 512:slot + 1024],
                                     Kv[64:128, p, kt * 128:(kt + 1) * 128],
                                     Qv[64:128, p, q0:q0 + 512],
                                     start=True, stop=True, tile_position=(64, 0))
                inc("t", last)
                assert C["t"] == t_S[u]
            elif ev[0] == "dnav":
                u = ev[1]
                qb, p, kt, o = uidx(u)
                G = qb * NPAIR + p
                tensor.wait_ge(ss, s_exp[u])
                if kt == 0 and o == 0 and G >= 1:
                    tensor.wait_ge(sv, v_dncp[G - 1])
                if kt == 0 and G >= 1:
                    tensor.wait_ge(sv, v_evac[G - 1])
                slot = (u % 2) * 1024
                pa = PP[:, slot:slot + 512]
                pb = PP[:, slot + 512:slot + 1024]
                rowA, rowB = 64 * o, 64 * o + 32
                st0, sp0 = (kt == 0), (kt == NKT - 1)
                tensor.matmul(psDN[rowA:rowA + 1, 0:512], ONESB[:, 0:1], pa,
                              start=st0, stop=sp0, tile_position=(0, rowA))
                tensor.matmul(psDN[rowB:rowB + 1, 0:512], ONESB[:, 0:1], pb,
                              start=st0, stop=sp0, tile_position=(0, rowB))
                tensor.matmul(psAV[0:64, o * 512:(o + 1) * 512],
                              VVv[:, kt, 128 * p:128 * p + 64], pa,
                              start=st0, stop=sp0, tile_position=(0, 0))
                last = tensor.matmul(psAV[64:128, o * 512:(o + 1) * 512],
                                     VVv[:, kt, 128 * p + 64:128 * p + 128], pb,
                                     start=st0, stop=sp0, tile_position=(0, 64))
                inc("t", last)
                assert C["t"] == t_dnav[u]
            else:
                k = ev[1]
                qb, tt, o = wo_groups[k]
                bank = wo_bank(k)
                if k == 0:
                    tensor.wait_ge(sx, x_wot)
                if k == 0 or k == 16:
                    tensor.wait_ge(sv, v_norm[qb * NPAIR + NPAIR - 1])
                elif k < 16:
                    tensor.wait_ge(sv, v_x1[k - 1])
                elif k >= 18:
                    tensor.wait_ge(sv, v_x1[k - 2])
                last = None
                for c in range(NCH):
                    last = tensor.matmul(
                        bank[:, :],
                        Oallv[:, c, qb * QBS + tt * 128:qb * QBS + (tt + 1) * 128],
                        woTv[:, c, o * 512:(o + 1) * 512],
                        start=(c == 0), stop=(c == NCH - 1))
                inc("t", last)
                assert C["t"] == t_wo[k]
    assert C["t"] == tc

    @blk.scalar
    def _(scalar):
        scalar.wait_ge(sg, negc_g)
        for u in range(NU):
            scalar.wait_ge(st_, t_S[u])
            if u >= 2:
                scalar.wait_ge(st_, t_dnav[u - 2])
            slot = (u % 2) * 1024
            inc("s", scalar.activation(PP[:, slot:slot + 1024],
                                       psS[:, slot:slot + 1024], AF.Exp,
                                       bias=NEGC[:, 0:1], scale=0.125))
            assert C["s"] == s_exp[u]

    @blk.vector
    def _(vector):
        for ev in vec_events:
            if ev[0] == "x1":
                k = ev[1]
                qb, tt, o = wo_groups[k]
                bank = wo_bank(k)
                vector.wait_ge(st_, t_wo[k])
                vector.wait_ge(sx, x_rw[(qb, tt)])
                if o == 0 and not (qb == 0 and tt < 2):
                    pq, ptt = (qb, tt - 2) if tt >= 2 else (qb - 1, tt + 6)
                    vector.wait_ge(sx, x_st[(pq, ptt)])
                half = (tt % 2) * 1024 + o * 512
                inc("v", vector.tensor_tensor(
                    F2[:, half:half + 512], bank[:, :],
                    F1[:, half:half + 512], ALU.add))
                assert C["v"] == v_x1[k]
            elif ev[0] == "evac":
                G = ev[1]
                qb, p = divmod(G, NPAIR)
                vector.wait_ge(st_, t_dnav[G * 32 + 31])
                inc("v", vector.tensor_copy(
                    Oallv[:, p, qb * QBS:(qb + 1) * QBS], psAV[:, 0:1024]))
                assert C["v"] == v_evac[G]
            elif ev[0] == "dncp":
                G = ev[1]
                inc("v", vector.tensor_copy(DENC[0:97, 0:512],
                                            psDN[0:97, 0:512]))
                assert C["v"] == v_dncp[G]
            elif ev[0] == "recip":
                G = ev[1]
                vector.wait_ge(sv, v_dncp[G])
                if G >= 1:
                    vector.wait_ge(sg, g_rdb[G - 1])
                inc("v", vector.reciprocal(DENR[0:97, 0:512],
                                           DENC[0:97, 0:512]))
                assert C["v"] == v_recip[G]
            else:
                G = ev[1]
                qb, p = divmod(G, NPAIR)
                vector.wait_ge(sg, g_rdb[G])
                osl = Oallv[:, p, qb * QBS:(qb + 1) * QBS]
                inc("v", vector.tensor_tensor(osl, osl, RDBn[:, 0:1024],
                                              ALU.mult))
                assert C["v"] == v_norm[G]
    assert C["v"] == vc

    @blk.gpsimd
    def _(gpsimd):
        for G in range(2 * NPAIR):
            gpsimd.wait_ge(sv, v_recip[G])
            if G >= 1:
                gpsimd.wait_ge(sv, v_norm[G - 1])
            for (row, p0, c0) in [(0, 0, 0), (64, 0, 512),
                                  (32, 64, 0), (96, 64, 512)]:
                inc("g", gpsimd.dma_start(
                    out=RDBn[p0:p0 + 64, c0:c0 + 512],
                    in_=DENR[row:row + 1, 0:512].unsqueeze(1)
                    .broadcast_to([1, 64, 512])), 16)
            assert C["g"] == g_rdb[G]
    assert C["g"] == gc

    @blk.sync
    def _(sync):
        sync.wait_ge(sg, P0g["woT"])
        sync.wait_ge(st_, VV_fill_done_t)
        inc("x", sync.dma_start(out=woTv[:, :, :],
                                in_=w_bf["woT"].rearrange("(c p) n -> p c n", p=128)), 16)
        assert C["x"] == x_wot
        for qb in range(NQB):
            for tt in range(8):
                s2 = tt % 2
                if qb == 0 and tt < 2:
                    sync.wait_ge(sv, VV_done_v)
                else:
                    pq, ptt = (qb, tt - 2) if tt >= 2 else (qb - 1, tt + 6)
                    sync.wait_ge(sv, v_x1[pq * 16 + ptt * 2 + 1])
                inc("x", sync.dma_start(
                    out=F1[:, s2 * 1024:(s2 + 1) * 1024],
                    in_=x_in[qb * QBS + tt * 128: qb * QBS + (tt + 1) * 128, :]), 16)
                assert C["x"] == x_rw[(qb, tt)]
                if tt >= 1:
                    sync.wait_ge(sv, v_x1[qb * 16 + (tt - 1) * 2 + 1])
                    inc("x", sync.dma_start(
                        out=x1_sp[qb * QBS + (tt - 1) * 128: qb * QBS + tt * 128, :],
                        in_=F2[:, ((tt - 1) % 2) * 1024:((tt - 1) % 2 + 1) * 1024]), 16)
                    assert C["x"] == x_st[(qb, tt - 1)]
            sync.wait_ge(sv, v_x1[qb * 16 + 7 * 2 + 1])
            inc("x", sync.dma_start(
                out=x1_sp[qb * QBS + 7 * 128: qb * QBS + 8 * 128, :],
                in_=F2[:, 1024:2048]), 16)
            assert C["x"] == x_st[(qb, 7)]
    assert C["x"] == xc

    psWO_ctx.__exit__(None, None, None)
    psDN_ctx.__exit__(None, None, None)
    psAV_ctx.__exit__(None, None, None)
    psS_ctx.__exit__(None, None, None)
    emit_canary([(sx, C["x"]), (sv, C["v"]), (ss, C["s"]), (st_, C["t"])])

    ps_t2_ctx = nc.psum_tensor("ps_t2", [128, 2048], BF16)
    ps_t2 = ps_t2_ctx.__enter__()
    ps_t2v = ps_t2[:].rearrange("p (k n) -> p k n", k=2)
    emit_ln_phase(x1_sp, H2Tv, ps_t2v,
                  sync_waits=[(sx, C["x"]), (sv, C["v"])],
                  vec_waits=[])
    ps_t2_ctx.__exit__(None, None, None)
    emit_canary([(sv, C["v"])])

    ps_a_ctx2 = nc.psum_tensor("ps_a2", [128, 1024], F32)
    ps_b_ctx2 = nc.psum_tensor("ps_b2", [128, 1024], F32)
    ps_e_ctx = nc.psum_tensor("ps_e", [128, 1024], F32)
    ps_f_ctx = nc.psum_tensor("ps_f", [128, 1024], F32)
    ps_a2 = ps_a_ctx2.__enter__()
    ps_b2 = ps_b_ctx2.__enter__()
    ps_e = ps_e_ctx.__enter__()
    ps_f = ps_f_ctx.__enter__()

    post_p4a_v = C["v"]

    @blk.sync
    def _(sync):
        sync.wait_ge(sv, post_p4a_v)
        sync.wait_ge(sg, P0_done_g)
        inc("x", sync.dma_start(out=WrTv[:, :, :],
                       in_=w_bf["wrT"].rearrange("(c p) n -> p c n", p=128)), 16)
        inc("x", sync.dma_start(out=WkTv[:, :, :],
                       in_=w_bf["wkT2"].rearrange("(c p) n -> p c n", p=128)), 16)
        inc("x", sync.dma_start(out=WvTv[:, :, :],
                                in_=w_bf["wvT2"].rearrange("(c p) n -> p c n", p=128)), 16)
    ffnw_x = C["x"]

    NSL = 4
    T0f, S0f, V0f, X0f = C["t"], C["s"], C["v"], C["x"]
    t_r, t_k, t_o = {}, {}, {}
    tc2 = T0f
    for s in range(NSL):
        for tt in range(4):
            tc2 += 1
            t_r[(s, tt)] = tc2
        for co in range(NCH):
            tc2 += 1
            t_k[(s, co)] = tc2
        for tt in range(4):
            tc2 += 1
            t_o[(s, tt)] = tc2
    s_r, s_k = {}, {}
    sc2 = S0f
    for s in range(NSL):
        for tt in range(4):
            sc2 += 1
            s_r[(s, tt)] = sc2
        for co in range(NCH):
            sc2 += 2
            s_k[(s, co)] = sc2
    v_xr, v_o = {}, {}
    vc2 = V0f
    for s in range(NSL):
        vc2 += 1 + 2 * NCH + (2 if s == 0 else 0)
        v_xr[s] = vc2
        for tt in range(4):
            vc2 += 1
            v_o[(s, tt)] = vc2
    x_x1, x_y = {}, {}
    xc2 = X0f
    for s in range(NSL):
        for tt in range(4):
            xc2 += 16
            x_x1[(s, tt)] = xc2
            if tt >= 1:
                xc2 += 16
                x_y[(s, tt - 1)] = xc2
        xc2 += 16
        x_y[(s, 3)] = xc2

    @blk.vector
    def _(vector):
        for s in range(NSL):
            if s >= 1:
                vector.wait_ge(st_, t_k[(s - 1, NCH - 1)])
            base = s * 512
            lo = 1 if s == 0 else 0
            if s == 0:
                inc("v", vector.tensor_tensor(
                    XKv[:, :, 1:512], H2Tv[:, :, 1:512], H2Tv[:, :, 0:511],
                    ALU.subtract))
            else:
                inc("v", vector.tensor_tensor(
                    XKv[:, :, 0:512], H2Tv[:, :, base:base + 512],
                    H2Tv[:, :, base - 1:base + 511], ALU.subtract))
            for c in range(NCH):
                sh = H2Tv[:, c, base + lo - 1:base + 511]
                inc("v", vector.scalar_tensor_tensor(
                    XRv[:, c, lo:512], XKv[:, c, lo:512], MU[:, c:c + 1], sh,
                    ALU.mult, ALU.add))
            for c in range(NCH):
                sh = H2Tv[:, c, base + lo - 1:base + 511]
                inc("v", vector.scalar_tensor_tensor(
                    XKv[:, c, lo:512], XKv[:, c, lo:512], MU[:, NCH + c:NCH + c + 1],
                    sh, ALU.mult, ALU.add))
            if s == 0:
                last = None
                for c in range(NCH):
                    vector.tensor_scalar(XRv[:, c, 0:1], H2Tv[:, c, 0:1],
                                         MU[:, c:c + 1], None, ALU.mult)
                    last = vector.tensor_scalar(XKv[:, c, 0:1], H2Tv[:, c, 0:1],
                                                MU[:, NCH + c:NCH + c + 1], None,
                                                ALU.mult)
                inc("v", last, 2)
            assert C["v"] == v_xr[s]
            for tt in range(4):
                vector.wait_ge(st_, t_o[(s, tt)])
                vector.wait_ge(sx, x_x1[(s, tt)])
                if tt >= 2:
                    vector.wait_ge(sx, x_y[(s, tt - 2)])
                s2 = tt % 2
                psl = ps_f if tt % 2 == 0 else ps_e
                vector.tensor_tensor(F2[:, s2 * 1024:(s2 + 1) * 1024],
                                     psl[:, :], Rv[:, tt, :], ALU.mult)
                inc("v", vector.tensor_tensor(
                    F2[:, s2 * 1024:(s2 + 1) * 1024],
                    F2[:, s2 * 1024:(s2 + 1) * 1024],
                    F1[:, s2 * 1024:(s2 + 1) * 1024], ALU.add))
                assert C["v"] == v_o[(s, tt)]
    assert C["v"] == vc2

    @blk.tensor
    def _(tensor):
        tensor.wait_ge(sx, ffnw_x)
        for s in range(NSL):
            tensor.wait_ge(sv, v_xr[s])
            for tt in range(4):
                sl = ps_a2 if tt % 2 == 0 else ps_b2
                if not (s == 0 and tt < 2):
                    pq, ptt = (s, tt - 2) if tt >= 2 else (s - 1, tt + 2)
                    tensor.wait_ge(ss, s_r[(pq, ptt)])
                last = None
                for o in range(2):
                    for c in range(NCH):
                        last = tensor.matmul(
                            sl[:, o * 512:(o + 1) * 512],
                            XRv[:, c, tt * 128:(tt + 1) * 128],
                            WrTv[:, c, o * 512:(o + 1) * 512],
                            start=(c == 0), stop=(c == NCH - 1))
                inc("t", last)
                assert C["t"] == t_r[(s, tt)]
            for co in range(NCH):
                sl = ps_e[:, 0:512] if co % 2 == 0 else ps_e[:, 512:1024]
                if co == 0 and s >= 1:
                    tensor.wait_ge(sv, v_o[(s - 1, 3)])
                if not (s == 0 and co < 2):
                    pq, pco = (s, co - 2) if co >= 2 else (s - 1, co + NCH - 2)
                    tensor.wait_ge(ss, s_k[(pq, pco)])
                last = None
                for c in range(NCH):
                    last = tensor.matmul(
                        sl, WkTv[:, c, co * 128:(co + 1) * 128],
                        XKv[:, c, 0:512],
                        start=(c == 0), stop=(c == NCH - 1))
                inc("t", last)
                assert C["t"] == t_k[(s, co)]
            for tt in range(4):
                psl = ps_f if tt % 2 == 0 else ps_e
                if tt == 0:
                    tensor.wait_ge(ss, s_k[(s, NCH - 1)])
                    if s >= 1:
                        tensor.wait_ge(sv, v_o[(s - 1, 2)])
                if tt == 1:
                    tensor.wait_ge(ss, s_k[(s, NCH - 1)])
                if tt >= 2:
                    tensor.wait_ge(sv, v_o[(s, tt - 2)])
                last = None
                for o in range(2):
                    for c in range(NCH):
                        last = tensor.matmul(
                            psl[:, o * 512:(o + 1) * 512],
                            kactv[:, c, tt * 128:(tt + 1) * 128],
                            WvTv[:, c, o * 512:(o + 1) * 512],
                            start=(c == 0), stop=(c == NCH - 1))
                inc("t", last)
                assert C["t"] == t_o[(s, tt)]
    assert C["t"] == tc2

    @blk.scalar
    def _(scalar):
        for s in range(NSL):
            if s >= 1:
                scalar.wait_ge(sv, v_o[(s - 1, 3)])
            for tt in range(4):
                sl = ps_a2 if tt % 2 == 0 else ps_b2
                scalar.wait_ge(st_, t_r[(s, tt)])
                inc("s", scalar.activation(Rv[:, tt, :], sl[:, :], AF.Sigmoid))
                assert C["s"] == s_r[(s, tt)]
            for co in range(NCH):
                sl = ps_e[:, 0:512] if co % 2 == 0 else ps_e[:, 512:1024]
                scalar.wait_ge(st_, t_k[(s, co)])
                if s >= 1 and co == 0:
                    scalar.wait_ge(st_, t_o[(s - 1, 3)])
                inc("s", scalar.activation(FK[:, :], sl, AF.Relu))
                inc("s", scalar.activation(kactv[:, co, :], FK[:, :], AF.Square))
                assert C["s"] == s_k[(s, co)]
    assert C["s"] == sc2

    @blk.sync
    def _(sync):
        for s in range(NSL):
            for tt in range(4):
                s2 = tt % 2
                if s == 0 and tt < 2:
                    sync.wait_ge(sv, post_p4a_v)
                else:
                    pq, ptt = (s, tt - 2) if tt >= 2 else (s - 1, tt + 2)
                    sync.wait_ge(sv, v_o[(pq, ptt)])
                r0 = s * 512 + tt * 128
                inc("x", sync.dma_start(out=F1[:, s2 * 1024:(s2 + 1) * 1024],
                                        in_=x1_sp[r0:r0 + 128, :]), 16)
                assert C["x"] == x_x1[(s, tt)]
                if tt >= 1:
                    sync.wait_ge(sv, v_o[(s, tt - 1)])
                    r1 = s * 512 + (tt - 1) * 128
                    inc("x", sync.dma_start(
                        out=y_out[r1:r1 + 128, :],
                        in_=F2[:, ((tt - 1) % 2) * 1024:((tt - 1) % 2 + 1) * 1024]), 16)
                    assert C["x"] == x_y[(s, tt - 1)]
            sync.wait_ge(sv, v_o[(s, 3)])
            r1 = s * 512 + 3 * 128
            inc("x", sync.dma_start(
                out=y_out[r1:r1 + 128, :],
                in_=F2[:, 1024:2048]), 16)
            assert C["x"] == x_y[(s, 3)]
        sync.wait_ge(sx, C["x"])
    assert C["x"] == xc2

    emit_canary([(sx, C["x"]), (sv, C["v"])])

    @blk.gpsimd
    def _(gpsimd):
        gpsimd.wait_ge(sg, C["g"])

    ps_f_ctx.__exit__(None, None, None)
    ps_e_ctx.__exit__(None, None, None)
    ps_b_ctx2.__exit__(None, None, None)
    ps_a_ctx2.__exit__(None, None, None)
    stack.close()
    return nc


_CACHE = {}


def _build_cached():
    if "nc" not in _CACHE:
        _CACHE["nc"] = build_nc()
    return _CACHE["nc"]


def kernel(L0_attn_x, L0_attn_wq, L0_attn_wk_a, L0_attn_wk_b, L0_attn_wv_a,
           L0_attn_wv_b, L0_attn_wo, L0_ffn_W_r, L0_ffn_W_k, L0_ffn_W_v,
           L0_ffn_mu_r, L0_ffn_mu_k):
    f32 = np.float32
    cT = lambda a: np.ascontiguousarray(np.asarray(a, dtype=f32).T)
    cos_t, sin_t = _rope_tables()
    shared = {
        "wqT": cT(L0_attn_wq), "wkaT": cT(L0_attn_wk_a), "wkbT": cT(L0_attn_wk_b),
        "wvaT": cT(L0_attn_wv_a), "wvbT": cT(L0_attn_wv_b), "woT": cT(L0_attn_wo),
        "wrT": cT(L0_ffn_W_r), "wkT2": cT(L0_ffn_W_k), "wvT2": cT(L0_ffn_W_v),
        "cos_in": cos_t, "sin_in": sin_t,
        "mur_in": np.ascontiguousarray(
            np.asarray(L0_ffn_mu_r, dtype=f32).reshape(NCH, 128).T),
        "muk_in": np.ascontiguousarray(
            np.asarray(L0_ffn_mu_k, dtype=f32).reshape(NCH, 128).T),
        "idb_in": np.eye(128, dtype=np.float32).astype(ml_dtypes.bfloat16),
        "idf_in": np.eye(128, dtype=np.float32),
    }
    x = np.asarray(L0_attn_x, dtype=f32)
    in_maps = []
    for b in range(B):
        m = dict(shared)
        m["x_in"] = np.ascontiguousarray(x[b])
        in_maps.append(m)
    nc = _build_cached()
    res = run_bass_kernel_spmd(nc, in_maps, core_ids=list(range(B)))
    out = np.stack([np.asarray(res.results[b]["y_out"]) for b in range(B)], axis=0)
    return out.astype(np.float32)


# version 11
# speedup vs baseline: 1.2337x; 1.2337x over previous
import numpy as np
import ml_dtypes
import concourse.bass as bass
import concourse.mybir as mybir
from concourse.bass_utils import run_bass_kernel_spmd

F32 = mybir.dt.float32
BF16 = mybir.dt.bfloat16
AF = mybir.ActivationFunctionType
ALU = mybir.AluOpType
AXX = mybir.AxisListType.X

B, T, D = 8, 2048, 1024
NH, HD, DR, LAT = 16, 64, 16, 256
NT = T // 128
NCH = D // 128
NKT = T // 128
NQB = 2
QBS = 1024
NPAIR = NH // 2
EPS = 1e-5

_WSPECS = [
    ("wqT", D, D), ("wkaT", D, LAT), ("wkbT", LAT, D),
    ("wvaT", D, LAT), ("wvbT", LAT, D), ("woT", D, D),
    ("wrT", D, D), ("wkT2", D, D), ("wvT2", D, D),
]


def _rope_tables():
    i64 = np.arange(0, HD, 2, dtype=np.float64)[: DR // 2]
    inv = 1.0 / 10000.0 ** (i64 / HD)
    ang = np.arange(NH, dtype=np.float64)[:, None] * inv[None, :]
    cos = np.cos(ang).astype(np.float32).reshape(NH * 8)
    sin = np.sin(ang).astype(np.float32).reshape(NH * 8)
    cos_t = np.ascontiguousarray(np.broadcast_to(cos[None, :], (128, NH * 8)))
    sin_t = np.ascontiguousarray(np.broadcast_to(sin[None, :], (128, NH * 8)))
    return cos_t, sin_t


def build_nc():
    nc = bass.Bass("TRN2", target_bir_lowering=False, debug=False,
                   detect_race_conditions=False)

    x_in = nc.dram_tensor("x_in", [T, D], F32, kind="ExternalInput").ap()
    w_in = {n: nc.dram_tensor(n, [r, c], F32, kind="ExternalInput").ap()
            for (n, r, c) in _WSPECS}
    cos_in = nc.dram_tensor("cos_in", [128, NT * 8], F32, kind="ExternalInput").ap()
    sin_in = nc.dram_tensor("sin_in", [128, NT * 8], F32, kind="ExternalInput").ap()
    mur_in = nc.dram_tensor("mur_in", [128, NCH], F32, kind="ExternalInput").ap()
    muk_in = nc.dram_tensor("muk_in", [128, NCH], F32, kind="ExternalInput").ap()
    idb_in = nc.dram_tensor("idb_in", [128, 128], BF16, kind="ExternalInput").ap()
    idf_in = nc.dram_tensor("idf_in", [128, 128], F32, kind="ExternalInput").ap()
    y_out = nc.dram_tensor("y_out", [T, D], F32, kind="ExternalOutput").ap()
    w_bf = {n: nc.dram_tensor(n + "_bf", [r, c], BF16, kind="Internal").ap()
            for (n, r, c) in _WSPECS}
    x1_sp = nc.dram_tensor("x1_sp", [T, D], F32, kind="Internal").ap()
    canary = nc.dram_tensor("canary", [1, 32], F32, kind="Internal").ap()

    from contextlib import ExitStack
    stack = ExitStack()

    def sb(name, shape, dt):
        return stack.enter_context(nc.sbuf_tensor(name, shape, dt))

    G1 = sb("G1", [128, 16384], BF16)
    G2 = sb("G2", [128, 16640], BF16)
    G3 = sb("G3", [128, 16384], BF16)
    G5 = sb("G5", [128, 16384], BF16)
    G6 = sb("G6", [128, 8192], BF16)
    G7 = sb("G7", [128, 2048], BF16)
    F1 = sb("F1", [128, 2048], F32)
    F2 = sb("F2", [128, 2048], F32)
    FK = sb("FK", [128, 512], F32)
    CS = sb("CSt", [128, 2 * NT * 8], F32)
    MU = sb("MUt", [128, 2 * NCH], F32)
    IDB = sb("IDB", [128, 128], BF16)
    IDF = sb("IDF", [128, 128], F32)
    SSUM = sb("SSUM", [128, 2], F32)
    SSQ = sb("SSQ", [128, 2], F32)
    MEAN = sb("MEAN", [128, 2], F32)
    VAR = sb("VAR", [128, 2], F32)
    RSTD = sb("RSTD", [128, 2], F32)
    RTMP = sb("RTMP", [128, 256], F32)
    NQT = sb("NQT", [128, 16], F32)
    NQA = sb("NQA", [128, 16], F32)
    NKA = sb("NKA", [128, 16], F32)
    CROW = sb("CROW", [1, 32], F32)
    NEGC = sb("NEGC", [128, 1], F32)
    PP = sb("PP", [128, 2048], BF16)
    RDBn = sb("RDBn", [128, 1024], F32)
    DENR = sb("DENR", [128, 512], F32)
    DENC = sb("DENC", [128, 512], F32)
    ONESB = sb("ONESB", [128, 1], BF16)
    WSF = sb("WSF", [128, 1024], F32)
    WSB = sb("WSB", [128, 1024], BF16)

    sx = stack.enter_context(nc.semaphore("sx"))
    sg = stack.enter_context(nc.semaphore("sg"))
    st_ = stack.enter_context(nc.semaphore("st"))
    sv = stack.enter_context(nc.semaphore("sv"))
    ss = stack.enter_context(nc.semaphore("ss"))
    scw = stack.enter_context(nc.semaphore("scw"))
    sldA = stack.enter_context(nc.semaphore("sldA"))
    sldB = stack.enter_context(nc.semaphore("sldB"))
    SEM = dict(x=sx, g=sg, t=st_, v=sv, s=ss)
    C = dict(x=0, g=0, t=0, v=0, s=0)

    def inc(key, instr, k=1):
        instr.then_inc(SEM[key], k)
        C[key] += k
        return C[key]

    CANARY_N = [0]

    def emit_canary(waits=()):
        idx = CANARY_N[0]
        CANARY_N[0] += 1

        @blk.gpsimd
        def _(gpsimd):
            for (sem, val) in waits:
                gpsimd.wait_ge(sem, val)
            inc("g", gpsimd.dma_start(out=canary[0:1, idx:idx + 1],
                                      in_=MU[0:1, 0:1]), 16)

    blk = stack.enter_context(nc.Block())

    HTv = G1[:].rearrange("p (c t) -> p c t", c=NCH)
    Qv = G1[:].rearrange("p (c t) -> p c t", c=NCH)
    QTv = G2[:, 0:16384].rearrange("p (i n) -> p i n", i=NT)
    KTv = QTv
    VVv = G2[:, 0:16384].rearrange("p (i n) -> p i n", i=NT)
    H2Tv = G2[:, 0:16384].rearrange("p (c t) -> p c t", c=NCH)
    Kv = G3[:].rearrange("p (c t) -> p c t", c=NCH)
    latKv = G6[:, 0:4096].rearrange("p (l t) -> p l t", l=2)
    latVv = G6[:, 4096:8192].rearrange("p (l t) -> p l t", l=2)
    woTv = G6[:].rearrange("p (c n) -> p c n", c=NCH)
    wqf = G5[:, 0:8192].rearrange("p (c n) -> p c n", c=NCH)
    wkas = G5[:, 8192:10240].rearrange("p (c n) -> p c n", c=NCH)
    wkbs = G5[:, 10240:12288].rearrange("p (l n) -> p l n", l=2)
    wvas = G5[:, 12288:14336].rearrange("p (c n) -> p c n", c=NCH)
    wvbs = G5[:, 14336:16384].rearrange("p (l n) -> p l n", l=2)
    Oallv = G5[:].rearrange("p (c n) -> p c n", c=NCH)
    XRv = G5[:, 0:4096].rearrange("p (c n) -> p c n", c=NCH)
    XKv = G5[:, 4096:8192].rearrange("p (c n) -> p c n", c=NCH)
    Rv = G5[:, 8192:12288].rearrange("p (i n) -> p i n", i=4)
    WrTv = G1[:, 0:8192].rearrange("p (c n) -> p c n", c=NCH)
    kactv = G1[:, 8192:12288].rearrange("p (c n) -> p c n", c=NCH)
    WkTv = G3[:, 0:8192].rearrange("p (c n) -> p c n", c=NCH)
    WvTv = G3[:, 8192:16384].rearrange("p (c n) -> p c n", c=NCH)
    cosv = CS[:, 0:NT * 8].rearrange("p (i f) -> p i f", i=NT)
    sinv = CS[:, NT * 8:].rearrange("p (i f) -> p i f", i=NT)
    G7v = G7[:].rearrange("p (k n) -> p k n", k=2)

    @blk.sync
    def _(sync):
        inc("x", sync.dma_start(out=IDB[:], in_=idb_in), 16)
        inc("x", sync.dma_start(out=IDF[:], in_=idf_in), 16)
        inc("x", sync.dma_start(out=CS[:, 0:NT * 8], in_=cos_in), 16)
        inc("x", sync.dma_start(out=CS[:, NT * 8:], in_=sin_in), 16)
        inc("x", sync.dma_start(out=MU[:, 0:NCH], in_=mur_in), 16)
        inc("x", sync.dma_start(out=MU[:, NCH:], in_=muk_in), 16)

    _wdims = {n: (r, c) for (n, r, c) in _WSPECS}

    def _chunks_of(names):
        out = []
        for n in names:
            r, c = _wdims[n]
            for p in range(r // 128):
                for h in range(0, c, 512):
                    out.append((n, p, h, min(512, c - h)))
        return out

    chunks = (_chunks_of(["wkaT", "wvaT", "wkbT", "wvbT", "wqT"])
              + _chunks_of(["woT", "wrT", "wkT2", "wvT2"]))
    NCK = len(chunks)
    N_WEAVE = NCK
    SCW_SMALLW = 24
    SCW_WQ = 40

    def _direct_dst(n, p, h, w):
        if n == "wkaT":
            return wkas[:, p, 0:w]
        if n == "wvaT":
            return wvas[:, p, 0:w]
        if n == "wkbT":
            return wkbs[:, p, h:h + w]
        if n == "wvbT":
            return wvbs[:, p, h:h + w]
        if n == "wqT":
            return wqf[:, p, h:h + w]
        return None

    bounce_idx = {}
    for j, (n, p, h, w) in enumerate(chunks):
        if n in ("woT", "wrT", "wkT2", "wvT2"):
            bounce_idx[j] = len(bounce_idx)
    g_store, P0g = {}, {}

    @blk.gpsimd
    def _(gpsimd):
        for j, (n, p, h, w) in enumerate(chunks):
            if j not in bounce_idx:
                continue
            bj = bounce_idx[j]
            gpsimd.wait_ge(scw, j + 1)
            if bj >= 1:
                gpsimd.wait_ge(sg, g_store[bj - 1])
            s2 = (bj % 2) * 512
            inc("g", gpsimd.dma_start(
                out=w_bf[n][p * 128:(p + 1) * 128, h:h + w],
                in_=WSB[:, s2:s2 + w]), 16)
            g_store[bj] = C["g"]
            if j + 1 == NCK or chunks[j + 1][0] != n:
                P0g[n] = C["g"]
    P0_done_g = C["g"]

    CAST_J = [0]
    LD_J = [0]

    def emit_casts(scalar, budget, limit=N_WEAVE):
        while CAST_J[0] < limit and budget > 0:
            j = CAST_J[0]
            while LD_J[0] < min(j + 2, limit):
                i = LD_J[0]
                n, p, h, w = chunks[i]
                if i >= 2:
                    scalar.wait_ge(scw, i - 1)
                s = (i % 2) * 512
                scalar.dma_start(out=WSF[:, s:s + w],
                                 in_=w_in[n][p * 128:(p + 1) * 128, h:h + w]
                                 ).then_inc(sldA if i % 2 == 0 else sldB, 16)
                LD_J[0] += 1
            n, p, h, w = chunks[j]
            scalar.wait_ge(sldA if j % 2 == 0 else sldB, 16 * (j // 2 + 1))
            dst = _direct_dst(n, p, h, w)
            if dst is None:
                bj = bounce_idx[j]
                if bj >= 2:
                    scalar.wait_ge(sg, g_store[bj - 2])
                dst = WSB[:, (bj % 2) * 512:(bj % 2) * 512 + w]
            s = (j % 2) * 512
            scalar.activation(dst, WSF[:, s:s + w],
                              AF.Copy).then_inc(scw, 1)
            CAST_J[0] += 1
            budget -= 1
    emit_canary([(sg, P0_done_g), (sx, 16 * 6)])

    def emit_ln_phase(src_hbm, dstv, ps_t, sync_waits, vec_waits,
                      scalar_extra=None):
        X0, V0, S0, T0 = C["x"], C["v"], C["s"], C["t"]

        @blk.sync
        def _(sync):
            for (sem, val) in sync_waits:
                sync.wait_ge(sem, val)
            for i in range(NT):
                if i >= 2:
                    sync.wait_ge(sv, V0 + 9 * (i - 2) + 8)
                s = i % 2
                inc("x", sync.dma_start(out=F1[:, s * 1024:(s + 1) * 1024],
                                        in_=src_hbm[i * 128:(i + 1) * 128, :]), 16)

        @blk.scalar
        def _(scalar):
            for i in range(NT):
                s = i % 2
                if scalar_extra:
                    scalar_extra(scalar, i)
                scalar.wait_ge(sx, X0 + 16 * (i + 1))
                if i >= 2:
                    scalar.wait_ge(sv, V0 + 9 * (i - 2) + 1)
                inc("s", scalar.activation(F2[:, s * 1024:(s + 1) * 1024],
                                           F1[:, s * 1024:(s + 1) * 1024], AF.Square))
                scalar.wait_ge(sv, V0 + 9 * i + 7)
                if i >= 2:
                    scalar.wait_ge(sv, V0 + 9 * (i - 2) + 8)
                inc("s", scalar.activation(RSTD[:, s:s + 1], VAR[:, s:s + 1], AF.Sqrt))

        @blk.vector
        def _(vector):
            for (sem, val) in vec_waits:
                vector.wait_ge(sem, val)
            for i in range(NT):
                s = i % 2
                vector.wait_ge(sx, X0 + 16 * (i + 1))
                vector.reduce_sum(SSUM[:, s:s + 1], F1[:, s * 1024:(s + 1) * 1024], axis=AXX)
                vector.wait_ge(ss, S0 + 2 * i + 1)
                inc("v", vector.reduce_sum(SSQ[:, s:s + 1], F2[:, s * 1024:(s + 1) * 1024], axis=AXX))
                vector.wait_ge(sv, V0 + 9 * i + 1)
                inc("v", vector.tensor_scalar(MEAN[:, s:s + 1], SSUM[:, s:s + 1], 1.0 / D, None, ALU.mult))
                inc("v", vector.tensor_scalar(VAR[:, s:s + 1], SSQ[:, s:s + 1], 1.0 / D, None, ALU.mult))
                vector.wait_ge(sv, V0 + 9 * i + 3)
                inc("v", vector.tensor_tensor(SSQ[:, s:s + 1], MEAN[:, s:s + 1], MEAN[:, s:s + 1], ALU.mult))
                vector.wait_ge(sv, V0 + 9 * i + 4)
                inc("v", vector.tensor_tensor(VAR[:, s:s + 1], VAR[:, s:s + 1], SSQ[:, s:s + 1], ALU.subtract))
                vector.wait_ge(sv, V0 + 9 * i + 5)
                inc("v", vector.tensor_scalar(VAR[:, s:s + 1], VAR[:, s:s + 1], EPS, None, ALU.add))
                vector.wait_ge(sv, V0 + 9 * i + 6)
                inc("v", vector.reciprocal(VAR[:, s:s + 1], VAR[:, s:s + 1]))
                vector.wait_ge(ss, S0 + 2 * i + 2)
                inc("v", vector.tensor_scalar(G7v[:, s, :], F1[:, s * 1024:(s + 1) * 1024],
                                              MEAN[:, s:s + 1], RSTD[:, s:s + 1],
                                              ALU.subtract, ALU.mult))
                vector.wait_ge(st_, T0 + i + 1)
                inc("v", vector.tensor_copy(
                    dstv[:, :, i * 128:(i + 1) * 128],
                    ps_t[:, s, :].rearrange("p (c n) -> p c n", c=NCH)))

        @blk.tensor
        def _(tensor):
            for i in range(NT):
                s = i % 2
                tensor.wait_ge(sv, V0 + 9 * i + 8)
                if i >= 2:
                    tensor.wait_ge(sv, V0 + 9 * (i - 2) + 9)
                last = None
                for c in range(NCH):
                    last = tensor.transpose(
                        ps_t[:, s, c * 128:(c + 1) * 128],
                        G7v[:, s, c * 128:(c + 1) * 128], IDB[:])
                inc("t", last)
        assert C["v"] == V0 + 9 * NT and C["s"] == S0 + 2 * NT and C["t"] == T0 + NT

    ps_a_ctx = nc.psum_tensor("ps_a", [128, 1024], F32)
    ps_b_ctx = nc.psum_tensor("ps_b", [128, 1024], F32)
    ps_a = ps_a_ctx.__enter__()
    ps_b = ps_b_ctx.__enter__()
    ps_t_ctx = nc.psum_tensor("ps_t", [128, 2048], BF16)
    ps_t = ps_t_ctx.__enter__()
    ps_tv = ps_t[:].rearrange("p (k n) -> p k n", k=2)

    emit_ln_phase(x_in, HTv, ps_tv, sync_waits=[], vec_waits=[],
                  scalar_extra=lambda scalar, i: emit_casts(scalar, 3))
    emit_canary([(sv, C["v"])])

    T0, S0 = C["t"], C["s"]
    fills = [(w, lc, tb) for w in range(2) for lc in range(2) for tb in range(4)]
    HT_ready_v = C["v"]

    @blk.tensor
    def _(tensor):
        tensor.wait_ge(scw, SCW_SMALLW)
        tensor.wait_ge(sv, HT_ready_v)
        for j, (which, lc, tb) in enumerate(fills):
            pss = (ps_a if j % 2 == 0 else ps_b)[:, 0:512]
            if j >= 2:
                tensor.wait_ge(ss, S0 + j - 1)
            wsl = wkas if which == 0 else wvas
            last = None
            for c in range(NCH):
                last = tensor.matmul(pss, wsl[:, c, lc * 128:(lc + 1) * 128],
                                     HTv[:, c, tb * 512:(tb + 1) * 512],
                                     start=(c == 0), stop=(c == NCH - 1))
            inc("t", last)

    @blk.scalar
    def _(scalar):
        for j, (which, lc, tb) in enumerate(fills):
            pss = (ps_a if j % 2 == 0 else ps_b)[:, 0:512]
            emit_casts(scalar, 1)
            scalar.wait_ge(st_, T0 + j + 1)
            dst = (latKv if which == 0 else latVv)[:, lc, tb * 512:(tb + 1) * 512]
            inc("s", scalar.activation(dst, pss, AF.Copy))
    assert C["t"] == T0 + 16 and C["s"] == S0 + 16
    emit_canary([(ss, C["s"])])

    def emit_qkt(src_is_q):
        T0b, S0b, V0b = C["t"], C["s"], C["v"]
        nacc = NCH if src_is_q else 2

        @blk.tensor
        def _(tensor):
            tensor.wait_ge(scw, SCW_WQ if src_is_q else SCW_SMALLW)
            for half in range(2):
                for i in range(NT):
                    j = half * NT + i
                    pss = (ps_a if j % 2 == 0 else ps_b)[:, 0:512]
                    if j >= 2:
                        tensor.wait_ge(ss, S0b + (j - 2) + 1)
                        tensor.wait_ge(sv, V0b + 6 * (j - 2) + 6)
                    else:
                        tensor.wait_ge(ss, S0b)
                        tensor.wait_ge(sv, V0b)
                    last = None
                    for a in range(nacc):
                        if src_is_q:
                            lhs = HTv[:, a, i * 128:(i + 1) * 128]
                            rhs = wqf[:, a, half * 512:(half + 1) * 512]
                        else:
                            lhs = latKv[:, a, i * 128:(i + 1) * 128]
                            rhs = wkbs[:, a, half * 512:(half + 1) * 512]
                        last = tensor.matmul(pss, lhs, rhs,
                                             start=(a == 0), stop=(a == nacc - 1))
                    inc("t", last)

        @blk.scalar
        def _(scalar):
            for half in range(2):
                for i in range(NT):
                    j = half * NT + i
                    pss = (ps_a if j % 2 == 0 else ps_b)[:, 0:512]
                    emit_casts(scalar, 1)
                    scalar.wait_ge(st_, T0b + j + 1)
                    inc("s", scalar.activation(
                        QTv[:, i, half * 512:(half + 1) * 512], pss, AF.Copy))

        @blk.vector
        def _(vector):
            for half in range(2):
                for i in range(NT):
                    j = half * NT + i
                    ps_full = ps_a if j % 2 == 0 else ps_b
                    vector.wait_ge(ss, S0b + j + 1)
                    ap = ps_full[:, 0:512].rearrange("p (h a b) -> p h a b", h=8, b=2)
                    xa = ap[:, :, 0:8, 0:1]
                    xb = ap[:, :, 0:8, 1:2]
                    cv = cosv[:, half * 8:(half + 1) * 8, :].unsqueeze(3)
                    sn = sinv[:, half * 8:(half + 1) * 8, :].unsqueeze(3)
                    ta = RTMP[:, 0:64].rearrange("p (h a b) -> p h a b", h=8, a=8)
                    tb = RTMP[:, 64:128].rearrange("p (h a b) -> p h a b", h=8, a=8)
                    tg = RTMP[:, 128:192].rearrange("p (h a b) -> p h a b", h=8, a=8)
                    td = RTMP[:, 192:256].rearrange("p (h a b) -> p h a b", h=8, a=8)
                    out = QTv[:, i, half * 512:(half + 1) * 512] \
                        .rearrange("p (h a b) -> p h a b", h=8, b=2)
                    oe = out[:, :, 0:8, 0:1]
                    oo = out[:, :, 0:8, 1:2]
                    vector.tensor_tensor(ta, xa, cv, ALU.mult)
                    vector.tensor_tensor(tb, xb, sn, ALU.mult)
                    vector.tensor_tensor(tg, xa, sn, ALU.mult)
                    inc("v", vector.tensor_tensor(td, xb, cv, ALU.mult))
                    vector.wait_ge(sv, C["v"])
                    vector.tensor_tensor(oe, ta, tb, ALU.subtract)
                    inc("v", vector.tensor_tensor(oo, tg, td, ALU.add), 5)
        assert C["t"] == T0b + 2 * NT and C["s"] == S0b + 2 * NT
        assert C["v"] == V0b + 6 * 2 * NT
        return V0b

    def emit_norms(acc, rope_v0):
        S0n, V0n = C["s"], C["v"]

        @blk.scalar
        def _(scalar):
            for i in range(NT):
                s = i % 2
                scalar.wait_ge(sv, rope_v0 + 6 * (NT + i) + 6)
                if i >= 2:
                    scalar.wait_ge(sv, V0n + 2 * (i - 2) + 1)
                inc("s", scalar.activation(F2[:, s * 1024:(s + 1) * 1024],
                                           QTv[:, i, :], AF.Square))

        @blk.vector
        def _(vector):
            for i in range(NT):
                s = i % 2
                vector.wait_ge(ss, S0n + i + 1)
                inc("v", vector.reduce_sum(
                    NQT[:].rearrange("p (h u) -> p h u", h=16, u=1),
                    F2[:, s * 1024:(s + 1) * 1024].rearrange("p (h u) -> p h u", h=16),
                    axis=AXX))
                vector.wait_ge(sv, C["v"])
                if i == 0:
                    inc("v", vector.tensor_copy(acc[:], NQT[:]))
                else:
                    inc("v", vector.tensor_tensor(acc[:], acc[:], NQT[:], ALU.max))

    def emit_big_transpose(srcv, dstv_, ready_v):
        T0c, V0c = C["t"], C["v"]

        @blk.tensor
        def _(tensor):
            tensor.wait_ge(sv, ready_v)
            for i in range(NT):
                s = i % 2
                if i >= 2:
                    tensor.wait_ge(sv, V0c + (i - 2) + 1)
                last = None
                for c in range(NCH):
                    last = tensor.transpose(ps_tv[:, s, c * 128:(c + 1) * 128],
                                            srcv[:, i, c * 128:(c + 1) * 128], IDB[:])
                inc("t", last)

        @blk.vector
        def _(vector):
            for i in range(NT):
                s = i % 2
                vector.wait_ge(st_, T0c + i + 1)
                inc("v", vector.tensor_copy(
                    dstv_[:, :, i * 128:(i + 1) * 128],
                    ps_tv[:, s, :].rearrange("p (c n) -> p c n", c=NCH)))
        assert C["t"] == T0c + NT and C["v"] == V0c + NT

    q_rope_v0 = emit_qkt(True)
    emit_canary([(sv, C["v"]), (ss, C["s"])])
    emit_norms(NQA, q_rope_v0)
    emit_big_transpose(QTv, Qv, C["v"])
    k_rope_v0 = emit_qkt(False)
    emit_canary([(sv, C["v"]), (ss, C["s"])])
    emit_norms(NKA, k_rope_v0)
    emit_big_transpose(KTv, Kv, C["v"])
    K_done_v = C["v"]

    T0v, V0v, S0v = C["t"], C["v"], C["s"]

    @blk.tensor
    def _(tensor):
        tensor.wait_ge(sv, K_done_v)
        tensor.wait_ge(ss, S0v)
        tensor.wait_ge(scw, SCW_SMALLW)
        for i in range(NT):
            for half in range(2):
                j = i * 2 + half
                pss = (ps_a if j % 2 == 0 else ps_b)[:, 0:512]
                if j >= 2:
                    tensor.wait_ge(sv, V0v + 1 + (j - 1))
                last = None
                for a in range(2):
                    last = tensor.matmul(pss, latVv[:, a, i * 128:(i + 1) * 128],
                                         wvbs[:, a, half * 512:(half + 1) * 512],
                                         start=(a == 0), stop=(a == 1))
                inc("t", last)

    @blk.vector
    def _(vector):
        vector.wait_ge(sv, K_done_v)
        inc("v", vector.memset(ONESB[:], 1.0))
        for i in range(NT):
            for half in range(2):
                j = i * 2 + half
                pss = (ps_a if j % 2 == 0 else ps_b)[:, 0:512]
                vector.wait_ge(st_, T0v + j + 1)
                inc("v", vector.tensor_copy(
                    VVv[:, i, half * 512:(half + 1) * 512], pss))
    VV_done_v = C["v"]
    VV_fill_done_t = C["t"]
    assert C["t"] == T0v + 32 and C["v"] == V0v + 33
    emit_canary([(sv, C["v"])])

    T0n, V0n, S0n = C["t"], C["v"], C["s"]

    @blk.tensor
    def _(tensor):
        tensor.wait_ge(sv, VV_done_v)
        inc("t", tensor.transpose(ps_a[0:16, 0:128], NQA[:], IDF[:]))
        inc("t", tensor.transpose(ps_b[0:16, 0:128], NKA[:], IDF[:]))

    @blk.vector
    def _(vector):
        vector.wait_ge(st_, T0n + 2)
        vector.reduce_max(NQT[0:16, 0:1], ps_a[0:16, 0:128], axis=AXX)
        inc("v", vector.reduce_max(NQT[0:16, 1:2], ps_b[0:16, 0:128], axis=AXX))
        vector.wait_ge(sv, C["v"])
        inc("v", vector.tensor_tensor(NQT[0:16, 2:3], NQT[0:16, 0:1],
                                      NQT[0:16, 1:2], ALU.mult))

    @blk.scalar
    def _(scalar):
        scalar.wait_ge(sv, V0n + 2)
        inc("s", scalar.activation(NQT[0:16, 3:4], NQT[0:16, 2:3], AF.Sqrt))

    @blk.vector
    def _(vector):
        vector.wait_ge(ss, S0n + 1)
        inc("v", vector.tensor_scalar(NQT[0:16, 4:5], NQT[0:16, 3:4],
                                      0.125, None, ALU.mult))

    @blk.tensor
    def _(tensor):
        tensor.wait_ge(sv, V0n + 3)
        inc("t", tensor.transpose(ps_a[0:1, 0:16], NQT[0:16, 4:5], IDF[0:16, 0:16]))

    @blk.vector
    def _(vector):
        vector.wait_ge(st_, T0n + 3)
        inc("v", vector.tensor_copy(CROW[0:1, 0:16], ps_a[0:1, 0:16]))
        vector.wait_ge(sv, C["v"])
        inc("v", vector.reduce_max(CROW[0:1, 16:17], CROW[0:1, 0:16], axis=AXX))
        vector.wait_ge(sv, C["v"])
        inc("v", vector.tensor_scalar(CROW[0:1, 17:18], CROW[0:1, 16:17],
                                      -60.0, 0.0, ALU.add, ALU.max))
        vector.wait_ge(sv, C["v"])
        inc("v", vector.tensor_scalar(CROW[0:1, 18:19], CROW[0:1, 17:18],
                                      -1.0, None, ALU.mult))

    @blk.gpsimd
    def _(gpsimd):
        gpsimd.wait_ge(sv, V0n + 7)
        inc("g", gpsimd.dma_start(
            out=NEGC[:], in_=CROW[0:1, 18:19].unsqueeze(1)
            .broadcast_to([1, 128, 1])), 16)
    negc_g = C["g"]
    emit_canary([(sg, negc_g)])

    @blk.scalar
    def _(scalar):
        emit_casts(scalar, NCK)
    assert CAST_J[0] == NCK

    ps_t_ctx.__exit__(None, None, None)
    ps_b_ctx.__exit__(None, None, None)
    ps_a_ctx.__exit__(None, None, None)

    psS_ctx = nc.psum_tensor("psS", [128, 2048], F32)
    psAV_ctx = nc.psum_tensor("psAV", [128, 1024], F32)
    psDN_ctx = nc.psum_tensor("psDN", [128, 512], F32)
    psWO_ctx = nc.psum_tensor("psWO", [128, 512], F32)
    psS = psS_ctx.__enter__()
    psAV = psAV_ctx.__enter__()
    psDN = psDN_ctx.__enter__()
    psWO = psWO_ctx.__enter__()

    NU = NQB * NPAIR * 32

    def uidx(u):
        G, s = divmod(u, 32)
        kt, o = divmod(s, 2)
        return G // NPAIR, G % NPAIR, kt, o

    wo_groups = [(qb, tt, o) for qb in range(NQB) for tt in range(8)
                 for o in range(2)]
    wo_before = {}
    for k in range(16):
        wo_before.setdefault(264 + 16 * k, []).append(k)
    wo_tail = list(range(16, 32))

    def wo_bank(k):
        if k < 16:
            return psWO
        return psWO if (k - 16) % 2 == 0 else psDN

    events = []
    for it in range(NU + 2):
        for k in wo_before.get(it, []):
            events.append(("wo", k))
        if it < NU:
            events.append(("S", it))
        if it >= 2:
            events.append(("dnav", it - 2))
    for k in wo_tail:
        events.append(("wo", k))

    t_S, t_dnav, t_wo = {}, {}, {}
    tc = C["t"]
    for ev in events:
        tc += 1
        if ev[0] == "S":
            t_S[ev[1]] = tc
        elif ev[0] == "dnav":
            t_dnav[ev[1]] = tc
        else:
            t_wo[ev[1]] = tc

    S0p = C["s"]
    s_exp = {u: S0p + u + 1 for u in range(NU)}

    vec_events = []
    evac_in_G = {G: [] for G in range(2 * NPAIR)}
    for k in range(16):
        evac_in_G[min((264 + 16 * k) // 32, 2 * NPAIR - 1)].append(k)
    for G in range(2 * NPAIR):
        for k in evac_in_G[G]:
            vec_events.append(("x1", k))
        vec_events.append(("evac", G))
        vec_events.append(("dncp", G))
        vec_events.append(("recip", G))
        vec_events.append(("norm", G))
    for k in wo_tail:
        vec_events.append(("x1", k))

    v_evac, v_dncp, v_recip, v_norm, v_x1 = {}, {}, {}, {}, {}
    vc = C["v"]
    for ev in vec_events:
        vc += 1
        if ev[0] == "evac":
            v_evac[ev[1]] = vc
        elif ev[0] == "dncp":
            v_dncp[ev[1]] = vc
        elif ev[0] == "recip":
            v_recip[ev[1]] = vc
        elif ev[0] == "norm":
            v_norm[ev[1]] = vc
        else:
            v_x1[ev[1]] = vc

    g_rdb = {}
    gc = C["g"]
    for G in range(2 * NPAIR):
        gc += 64
        g_rdb[G] = gc

    x_wot = C["x"] + 16
    x_rw, x_st = {}, {}
    xc = x_wot
    for qb in range(NQB):
        for tt in range(8):
            xc += 16
            x_rw[(qb, tt)] = xc
            if tt >= 1:
                xc += 16
                x_st[(qb, tt - 1)] = xc
        xc += 16
        x_st[(qb, 7)] = xc

    @blk.tensor
    def _(tensor):
        for ev in events:
            if ev[0] == "S":
                u = ev[1]
                qb, p, kt, o = uidx(u)
                if u >= 2:
                    tensor.wait_ge(ss, s_exp[u - 2])
                else:
                    tensor.wait_ge(sv, VV_done_v)
                    tensor.wait_ge(sg, negc_g)
                slot = (u % 2) * 1024
                q0 = qb * QBS + o * 512
                tensor.matmul(psS[:, slot:slot + 512],
                              Kv[0:64, p, kt * 128:(kt + 1) * 128],
                              Qv[0:64, p, q0:q0 + 512],
                              start=True, stop=True, tile_position=(0, 0))
                last = tensor.matmul(psS[:, slot + 512:slot + 1024],
                                     Kv[64:128, p, kt * 128:(kt + 1) * 128],
                                     Qv[64:128, p, q0:q0 + 512],
                                     start=True, stop=True, tile_position=(64, 0))
                inc("t", last)
                assert C["t"] == t_S[u]
            elif ev[0] == "dnav":
                u = ev[1]
                qb, p, kt, o = uidx(u)
                G = qb * NPAIR + p
                tensor.wait_ge(ss, s_exp[u])
                if kt == 0 and o == 0 and G >= 1:
                    tensor.wait_ge(sv, v_dncp[G - 1])
                if kt == 0 and G >= 1:
                    tensor.wait_ge(sv, v_evac[G - 1])
                slot = (u % 2) * 1024
                pa = PP[:, slot:slot + 512]
                pb = PP[:, slot + 512:slot + 1024]
                rowA, rowB = 64 * o, 64 * o + 32
                st0, sp0 = (kt == 0), (kt == NKT - 1)
                tensor.matmul(psDN[rowA:rowA + 1, 0:512], ONESB[:, 0:1], pa,
                              start=st0, stop=sp0, tile_position=(0, rowA))
                tensor.matmul(psDN[rowB:rowB + 1, 0:512], ONESB[:, 0:1], pb,
                              start=st0, stop=sp0, tile_position=(0, rowB))
                tensor.matmul(psAV[0:64, o * 512:(o + 1) * 512],
                              VVv[:, kt, 128 * p:128 * p + 64], pa,
                              start=st0, stop=sp0, tile_position=(0, 0))
                last = tensor.matmul(psAV[64:128, o * 512:(o + 1) * 512],
                                     VVv[:, kt, 128 * p + 64:128 * p + 128], pb,
                                     start=st0, stop=sp0, tile_position=(0, 64))
                inc("t", last)
                assert C["t"] == t_dnav[u]
            else:
                k = ev[1]
                qb, tt, o = wo_groups[k]
                bank = wo_bank(k)
                if k == 0:
                    tensor.wait_ge(sx, x_wot)
                if k == 0 or k == 16:
                    tensor.wait_ge(sv, v_norm[qb * NPAIR + NPAIR - 1])
                elif k < 16:
                    tensor.wait_ge(sv, v_x1[k - 1])
                elif k >= 18:
                    tensor.wait_ge(sv, v_x1[k - 2])
                last = None
                for c in range(NCH):
                    last = tensor.matmul(
                        bank[:, :],
                        Oallv[:, c, qb * QBS + tt * 128:qb * QBS + (tt + 1) * 128],
                        woTv[:, c, o * 512:(o + 1) * 512],
                        start=(c == 0), stop=(c == NCH - 1))
                inc("t", last)
                assert C["t"] == t_wo[k]
    assert C["t"] == tc

    @blk.scalar
    def _(scalar):
        scalar.wait_ge(sg, negc_g)
        for u in range(NU):
            scalar.wait_ge(st_, t_S[u])
            if u >= 2:
                scalar.wait_ge(st_, t_dnav[u - 2])
            slot = (u % 2) * 1024
            inc("s", scalar.activation(PP[:, slot:slot + 1024],
                                       psS[:, slot:slot + 1024], AF.Exp,
                                       bias=NEGC[:, 0:1], scale=0.125))
            assert C["s"] == s_exp[u]

    @blk.vector
    def _(vector):
        for ev in vec_events:
            if ev[0] == "x1":
                k = ev[1]
                qb, tt, o = wo_groups[k]
                bank = wo_bank(k)
                vector.wait_ge(st_, t_wo[k])
                vector.wait_ge(sx, x_rw[(qb, tt)])
                if o == 0 and not (qb == 0 and tt < 2):
                    pq, ptt = (qb, tt - 2) if tt >= 2 else (qb - 1, tt + 6)
                    vector.wait_ge(sx, x_st[(pq, ptt)])
                half = (tt % 2) * 1024 + o * 512
                inc("v", vector.tensor_tensor(
                    F2[:, half:half + 512], bank[:, :],
                    F1[:, half:half + 512], ALU.add))
                assert C["v"] == v_x1[k]
            elif ev[0] == "evac":
                G = ev[1]
                qb, p = divmod(G, NPAIR)
                vector.wait_ge(st_, t_dnav[G * 32 + 31])
                inc("v", vector.tensor_copy(
                    Oallv[:, p, qb * QBS:(qb + 1) * QBS], psAV[:, 0:1024]))
                assert C["v"] == v_evac[G]
            elif ev[0] == "dncp":
                G = ev[1]
                inc("v", vector.tensor_copy(DENC[0:97, 0:512],
                                            psDN[0:97, 0:512]))
                assert C["v"] == v_dncp[G]
            elif ev[0] == "recip":
                G = ev[1]
                vector.wait_ge(sv, v_dncp[G])
                if G >= 1:
                    vector.wait_ge(sg, g_rdb[G - 1])
                inc("v", vector.reciprocal(DENR[0:97, 0:512],
                                           DENC[0:97, 0:512]))
                assert C["v"] == v_recip[G]
            else:
                G = ev[1]
                qb, p = divmod(G, NPAIR)
                vector.wait_ge(sg, g_rdb[G])
                osl = Oallv[:, p, qb * QBS:(qb + 1) * QBS]
                inc("v", vector.tensor_tensor(osl, osl, RDBn[:, 0:1024],
                                              ALU.mult))
                assert C["v"] == v_norm[G]
    assert C["v"] == vc

    @blk.gpsimd
    def _(gpsimd):
        for G in range(2 * NPAIR):
            gpsimd.wait_ge(sv, v_recip[G])
            if G >= 1:
                gpsimd.wait_ge(sv, v_norm[G - 1])
            for (row, p0, c0) in [(0, 0, 0), (64, 0, 512),
                                  (32, 64, 0), (96, 64, 512)]:
                inc("g", gpsimd.dma_start(
                    out=RDBn[p0:p0 + 64, c0:c0 + 512],
                    in_=DENR[row:row + 1, 0:512].unsqueeze(1)
                    .broadcast_to([1, 64, 512])), 16)
            assert C["g"] == g_rdb[G]
    assert C["g"] == gc

    @blk.sync
    def _(sync):
        sync.wait_ge(sg, P0g["woT"])
        sync.wait_ge(st_, VV_fill_done_t)
        inc("x", sync.dma_start(out=woTv[:, :, :],
                                in_=w_bf["woT"].rearrange("(c p) n -> p c n", p=128)), 16)
        assert C["x"] == x_wot
        for qb in range(NQB):
            for tt in range(8):
                s2 = tt % 2
                if qb == 0 and tt < 2:
                    sync.wait_ge(sv, VV_done_v)
                else:
                    pq, ptt = (qb, tt - 2) if tt >= 2 else (qb - 1, tt + 6)
                    sync.wait_ge(sv, v_x1[pq * 16 + ptt * 2 + 1])
                inc("x", sync.dma_start(
                    out=F1[:, s2 * 1024:(s2 + 1) * 1024],
                    in_=x_in[qb * QBS + tt * 128: qb * QBS + (tt + 1) * 128, :]), 16)
                assert C["x"] == x_rw[(qb, tt)]
                if tt >= 1:
                    sync.wait_ge(sv, v_x1[qb * 16 + (tt - 1) * 2 + 1])
                    inc("x", sync.dma_start(
                        out=x1_sp[qb * QBS + (tt - 1) * 128: qb * QBS + tt * 128, :],
                        in_=F2[:, ((tt - 1) % 2) * 1024:((tt - 1) % 2 + 1) * 1024]), 16)
                    assert C["x"] == x_st[(qb, tt - 1)]
            sync.wait_ge(sv, v_x1[qb * 16 + 7 * 2 + 1])
            inc("x", sync.dma_start(
                out=x1_sp[qb * QBS + 7 * 128: qb * QBS + 8 * 128, :],
                in_=F2[:, 1024:2048]), 16)
            assert C["x"] == x_st[(qb, 7)]
    assert C["x"] == xc

    psWO_ctx.__exit__(None, None, None)
    psDN_ctx.__exit__(None, None, None)
    psAV_ctx.__exit__(None, None, None)
    psS_ctx.__exit__(None, None, None)
    emit_canary([(sx, C["x"]), (sv, C["v"]), (ss, C["s"]), (st_, C["t"])])

    ps_t2_ctx = nc.psum_tensor("ps_t2", [128, 2048], BF16)
    ps_t2 = ps_t2_ctx.__enter__()
    ps_t2v = ps_t2[:].rearrange("p (k n) -> p k n", k=2)
    emit_ln_phase(x1_sp, H2Tv, ps_t2v,
                  sync_waits=[(sx, C["x"]), (sv, C["v"])],
                  vec_waits=[])
    ps_t2_ctx.__exit__(None, None, None)
    emit_canary([(sv, C["v"])])

    ps_a_ctx2 = nc.psum_tensor("ps_a2", [128, 1024], F32)
    ps_b_ctx2 = nc.psum_tensor("ps_b2", [128, 1024], F32)
    ps_e_ctx = nc.psum_tensor("ps_e", [128, 1024], F32)
    ps_f_ctx = nc.psum_tensor("ps_f", [128, 1024], F32)
    ps_a2 = ps_a_ctx2.__enter__()
    ps_b2 = ps_b_ctx2.__enter__()
    ps_e = ps_e_ctx.__enter__()
    ps_f = ps_f_ctx.__enter__()

    post_p4a_v = C["v"]

    @blk.sync
    def _(sync):
        sync.wait_ge(sv, post_p4a_v)
        sync.wait_ge(sg, P0_done_g)
        inc("x", sync.dma_start(out=WrTv[:, :, :],
                       in_=w_bf["wrT"].rearrange("(c p) n -> p c n", p=128)), 16)
        inc("x", sync.dma_start(out=WkTv[:, :, :],
                       in_=w_bf["wkT2"].rearrange("(c p) n -> p c n", p=128)), 16)
        inc("x", sync.dma_start(out=WvTv[:, :, :],
                                in_=w_bf["wvT2"].rearrange("(c p) n -> p c n", p=128)), 16)
    ffnw_x = C["x"]

    NSL = 4
    T0f, S0f, V0f, X0f = C["t"], C["s"], C["v"], C["x"]
    t_r, t_k, t_o = {}, {}, {}
    tc2 = T0f
    for s in range(NSL):
        for tt in range(4):
            tc2 += 1
            t_r[(s, tt)] = tc2
        for co in range(NCH):
            tc2 += 1
            t_k[(s, co)] = tc2
        for tt in range(4):
            tc2 += 1
            t_o[(s, tt)] = tc2
    s_r, s_k = {}, {}
    sc2 = S0f
    for s in range(NSL):
        for tt in range(4):
            sc2 += 1
            s_r[(s, tt)] = sc2
        for co in range(NCH):
            sc2 += 2
            s_k[(s, co)] = sc2
    v_xr, v_o = {}, {}
    vc2 = V0f
    for s in range(NSL):
        vc2 += 1 + 2 * NCH + (2 if s == 0 else 0)
        v_xr[s] = vc2
        for tt in range(4):
            vc2 += 1
            v_o[(s, tt)] = vc2
    x_x1, x_y = {}, {}
    xc2 = X0f
    for s in range(NSL):
        for tt in range(4):
            xc2 += 16
            x_x1[(s, tt)] = xc2
            if tt >= 1:
                xc2 += 16
                x_y[(s, tt - 1)] = xc2
        xc2 += 16
        x_y[(s, 3)] = xc2

    @blk.vector
    def _(vector):
        for s in range(NSL):
            if s >= 1:
                vector.wait_ge(st_, t_k[(s - 1, NCH - 1)])
            base = s * 512
            lo = 1 if s == 0 else 0
            if s == 0:
                inc("v", vector.tensor_tensor(
                    XKv[:, :, 1:512], H2Tv[:, :, 1:512], H2Tv[:, :, 0:511],
                    ALU.subtract))
            else:
                inc("v", vector.tensor_tensor(
                    XKv[:, :, 0:512], H2Tv[:, :, base:base + 512],
                    H2Tv[:, :, base - 1:base + 511], ALU.subtract))
            for c in range(NCH):
                sh = H2Tv[:, c, base + lo - 1:base + 511]
                inc("v", vector.scalar_tensor_tensor(
                    XRv[:, c, lo:512], XKv[:, c, lo:512], MU[:, c:c + 1], sh,
                    ALU.mult, ALU.add))
            for c in range(NCH):
                sh = H2Tv[:, c, base + lo - 1:base + 511]
                inc("v", vector.scalar_tensor_tensor(
                    XKv[:, c, lo:512], XKv[:, c, lo:512], MU[:, NCH + c:NCH + c + 1],
                    sh, ALU.mult, ALU.add))
            if s == 0:
                last = None
                for c in range(NCH):
                    vector.tensor_scalar(XRv[:, c, 0:1], H2Tv[:, c, 0:1],
                                         MU[:, c:c + 1], None, ALU.mult)
                    last = vector.tensor_scalar(XKv[:, c, 0:1], H2Tv[:, c, 0:1],
                                                MU[:, NCH + c:NCH + c + 1], None,
                                                ALU.mult)
                inc("v", last, 2)
            assert C["v"] == v_xr[s]
            for tt in range(4):
                vector.wait_ge(st_, t_o[(s, tt)])
                vector.wait_ge(sx, x_x1[(s, tt)])
                if tt >= 2:
                    vector.wait_ge(sx, x_y[(s, tt - 2)])
                s2 = tt % 2
                psl = ps_f if tt % 2 == 0 else ps_e
                vector.tensor_tensor(F2[:, s2 * 1024:(s2 + 1) * 1024],
                                     psl[:, :], Rv[:, tt, :], ALU.mult)
                inc("v", vector.tensor_tensor(
                    F2[:, s2 * 1024:(s2 + 1) * 1024],
                    F2[:, s2 * 1024:(s2 + 1) * 1024],
                    F1[:, s2 * 1024:(s2 + 1) * 1024], ALU.add))
                assert C["v"] == v_o[(s, tt)]
    assert C["v"] == vc2

    @blk.tensor
    def _(tensor):
        tensor.wait_ge(sx, ffnw_x)
        for s in range(NSL):
            tensor.wait_ge(sv, v_xr[s])
            for tt in range(4):
                sl = ps_a2 if tt % 2 == 0 else ps_b2
                if not (s == 0 and tt < 2):
                    pq, ptt = (s, tt - 2) if tt >= 2 else (s - 1, tt + 2)
                    tensor.wait_ge(ss, s_r[(pq, ptt)])
                last = None
                for o in range(2):
                    for c in range(NCH):
                        last = tensor.matmul(
                            sl[:, o * 512:(o + 1) * 512],
                            XRv[:, c, tt * 128:(tt + 1) * 128],
                            WrTv[:, c, o * 512:(o + 1) * 512],
                            start=(c == 0), stop=(c == NCH - 1))
                inc("t", last)
                assert C["t"] == t_r[(s, tt)]
            for co in range(NCH):
                sl = ps_e[:, 0:512] if co % 2 == 0 else ps_e[:, 512:1024]
                if co == 0 and s >= 1:
                    tensor.wait_ge(sv, v_o[(s - 1, 3)])
                if not (s == 0 and co < 2):
                    pq, pco = (s, co - 2) if co >= 2 else (s - 1, co + NCH - 2)
                    tensor.wait_ge(ss, s_k[(pq, pco)])
                last = None
                for c in range(NCH):
                    last = tensor.matmul(
                        sl, WkTv[:, c, co * 128:(co + 1) * 128],
                        XKv[:, c, 0:512],
                        start=(c == 0), stop=(c == NCH - 1))
                inc("t", last)
                assert C["t"] == t_k[(s, co)]
            for tt in range(4):
                psl = ps_f if tt % 2 == 0 else ps_e
                if tt == 0:
                    tensor.wait_ge(ss, s_k[(s, NCH - 1)])
                    if s >= 1:
                        tensor.wait_ge(sv, v_o[(s - 1, 2)])
                if tt == 1:
                    tensor.wait_ge(ss, s_k[(s, NCH - 1)])
                if tt >= 2:
                    tensor.wait_ge(sv, v_o[(s, tt - 2)])
                last = None
                for o in range(2):
                    for c in range(NCH):
                        last = tensor.matmul(
                            psl[:, o * 512:(o + 1) * 512],
                            kactv[:, c, tt * 128:(tt + 1) * 128],
                            WvTv[:, c, o * 512:(o + 1) * 512],
                            start=(c == 0), stop=(c == NCH - 1))
                inc("t", last)
                assert C["t"] == t_o[(s, tt)]
    assert C["t"] == tc2

    @blk.scalar
    def _(scalar):
        for s in range(NSL):
            if s >= 1:
                scalar.wait_ge(sv, v_o[(s - 1, 3)])
            for tt in range(4):
                sl = ps_a2 if tt % 2 == 0 else ps_b2
                scalar.wait_ge(st_, t_r[(s, tt)])
                inc("s", scalar.activation(Rv[:, tt, :], sl[:, :], AF.Sigmoid))
                assert C["s"] == s_r[(s, tt)]
            for co in range(NCH):
                sl = ps_e[:, 0:512] if co % 2 == 0 else ps_e[:, 512:1024]
                scalar.wait_ge(st_, t_k[(s, co)])
                if s >= 1 and co == 0:
                    scalar.wait_ge(st_, t_o[(s - 1, 3)])
                inc("s", scalar.activation(FK[:, :], sl, AF.Relu))
                inc("s", scalar.activation(kactv[:, co, :], FK[:, :], AF.Square))
                assert C["s"] == s_k[(s, co)]
    assert C["s"] == sc2

    @blk.sync
    def _(sync):
        for s in range(NSL):
            for tt in range(4):
                s2 = tt % 2
                if s == 0 and tt < 2:
                    sync.wait_ge(sv, post_p4a_v)
                else:
                    pq, ptt = (s, tt - 2) if tt >= 2 else (s - 1, tt + 2)
                    sync.wait_ge(sv, v_o[(pq, ptt)])
                r0 = s * 512 + tt * 128
                inc("x", sync.dma_start(out=F1[:, s2 * 1024:(s2 + 1) * 1024],
                                        in_=x1_sp[r0:r0 + 128, :]), 16)
                assert C["x"] == x_x1[(s, tt)]
                if tt >= 1:
                    sync.wait_ge(sv, v_o[(s, tt - 1)])
                    r1 = s * 512 + (tt - 1) * 128
                    inc("x", sync.dma_start(
                        out=y_out[r1:r1 + 128, :],
                        in_=F2[:, ((tt - 1) % 2) * 1024:((tt - 1) % 2 + 1) * 1024]), 16)
                    assert C["x"] == x_y[(s, tt - 1)]
            sync.wait_ge(sv, v_o[(s, 3)])
            r1 = s * 512 + 3 * 128
            inc("x", sync.dma_start(
                out=y_out[r1:r1 + 128, :],
                in_=F2[:, 1024:2048]), 16)
            assert C["x"] == x_y[(s, 3)]
        sync.wait_ge(sx, C["x"])
    assert C["x"] == xc2

    emit_canary([(sx, C["x"]), (sv, C["v"])])

    @blk.gpsimd
    def _(gpsimd):
        gpsimd.wait_ge(sg, C["g"])

    ps_f_ctx.__exit__(None, None, None)
    ps_e_ctx.__exit__(None, None, None)
    ps_b_ctx2.__exit__(None, None, None)
    ps_a_ctx2.__exit__(None, None, None)
    stack.close()
    return nc


_CACHE = {}


def _build_cached():
    if "nc" not in _CACHE:
        _CACHE["nc"] = build_nc()
    return _CACHE["nc"]


def kernel(L0_attn_x, L0_attn_wq, L0_attn_wk_a, L0_attn_wk_b, L0_attn_wv_a,
           L0_attn_wv_b, L0_attn_wo, L0_ffn_W_r, L0_ffn_W_k, L0_ffn_W_v,
           L0_ffn_mu_r, L0_ffn_mu_k):
    f32 = np.float32
    cT = lambda a: np.ascontiguousarray(np.asarray(a, dtype=f32).T)
    cos_t, sin_t = _rope_tables()
    shared = {
        "wqT": cT(L0_attn_wq), "wkaT": cT(L0_attn_wk_a), "wkbT": cT(L0_attn_wk_b),
        "wvaT": cT(L0_attn_wv_a), "wvbT": cT(L0_attn_wv_b), "woT": cT(L0_attn_wo),
        "wrT": cT(L0_ffn_W_r), "wkT2": cT(L0_ffn_W_k), "wvT2": cT(L0_ffn_W_v),
        "cos_in": cos_t, "sin_in": sin_t,
        "mur_in": np.ascontiguousarray(
            np.asarray(L0_ffn_mu_r, dtype=f32).reshape(NCH, 128).T),
        "muk_in": np.ascontiguousarray(
            np.asarray(L0_ffn_mu_k, dtype=f32).reshape(NCH, 128).T),
        "idb_in": np.eye(128, dtype=np.float32).astype(ml_dtypes.bfloat16),
        "idf_in": np.eye(128, dtype=np.float32),
    }
    x = np.asarray(L0_attn_x, dtype=f32)
    in_maps = []
    for b in range(B):
        m = dict(shared)
        m["x_in"] = np.ascontiguousarray(x[b])
        in_maps.append(m)
    nc = _build_cached()
    res = run_bass_kernel_spmd(nc, in_maps, core_ids=list(range(B)))
    out = np.stack([np.asarray(res.results[b]["y_out"]) for b in range(B)], axis=0)
    return out.astype(np.float32)


# version 22
# speedup vs baseline: 1.4292x; 1.1585x over previous
import numpy as np
import ml_dtypes
import concourse.bass as bass
import concourse.mybir as mybir
from concourse.bass_utils import run_bass_kernel_spmd

F32 = mybir.dt.float32
BF16 = mybir.dt.bfloat16
AF = mybir.ActivationFunctionType
ALU = mybir.AluOpType
AXX = mybir.AxisListType.X

B, T, D = 8, 2048, 1024
NH, HD, DR, LAT = 16, 64, 16, 256
NT = T // 128
NCH = D // 128
NKT = T // 128
NQB = 2
QBS = 1024
NPAIR = NH // 2
EPS = 1e-5

_WSPECS = [
    ("wqT", D, D), ("wkaT", D, LAT), ("wkbT", LAT, D),
    ("wvaT", D, LAT), ("wvbT", LAT, D), ("woT", D, D),
    ("wrT", D, D), ("wkT2", D, D), ("wvT2", D, D),
]


def _rope_tables():
    i64 = np.arange(0, HD, 2, dtype=np.float64)[: DR // 2]
    inv = 1.0 / 10000.0 ** (i64 / HD)
    ang = np.arange(NH, dtype=np.float64)[:, None] * inv[None, :]
    cos = np.cos(ang).astype(np.float32).reshape(NH * 8)
    sin = np.sin(ang).astype(np.float32).reshape(NH * 8)
    cos_t = np.ascontiguousarray(np.broadcast_to(cos[None, :], (128, NH * 8)))
    sin_t = np.ascontiguousarray(np.broadcast_to(sin[None, :], (128, NH * 8)))
    return cos_t, sin_t


def build_nc():
    nc = bass.Bass("TRN2", target_bir_lowering=False, debug=False,
                   detect_race_conditions=False)

    x_in = nc.dram_tensor("x_in", [T, D], F32, kind="ExternalInput").ap()
    w_in = {n: nc.dram_tensor(n, [r, c], F32, kind="ExternalInput").ap()
            for (n, r, c) in _WSPECS}
    cos_in = nc.dram_tensor("cos_in", [128, NT * 8], F32, kind="ExternalInput").ap()
    sin_in = nc.dram_tensor("sin_in", [128, NT * 8], F32, kind="ExternalInput").ap()
    mur_in = nc.dram_tensor("mur_in", [128, NCH], F32, kind="ExternalInput").ap()
    muk_in = nc.dram_tensor("muk_in", [128, NCH], F32, kind="ExternalInput").ap()
    idb_in = nc.dram_tensor("idb_in", [128, 128], BF16, kind="ExternalInput").ap()
    idf_in = nc.dram_tensor("idf_in", [128, 128], F32, kind="ExternalInput").ap()
    y_out = nc.dram_tensor("y_out", [T, D], F32, kind="ExternalOutput").ap()
    w_bf = {n: nc.dram_tensor(n + "_bf", [r, c], BF16, kind="Internal").ap()
            for (n, r, c) in _WSPECS}
    x1_sp = nc.dram_tensor("x1_sp", [T, D], F32, kind="Internal").ap()
    canary = nc.dram_tensor("canary", [1, 32], F32, kind="Internal").ap()

    from contextlib import ExitStack
    stack = ExitStack()

    def sb(name, shape, dt):
        return stack.enter_context(nc.sbuf_tensor(name, shape, dt))

    G1 = sb("G1", [128, 16384], BF16)
    G2 = sb("G2", [128, 16640], BF16)
    G3 = sb("G3", [128, 16384], BF16)
    G5 = sb("G5", [128, 16384], BF16)
    G6 = sb("G6", [128, 8192], BF16)
    G7 = sb("G7", [128, 2048], BF16)
    F1 = sb("F1", [128, 2048], F32)
    F2 = sb("F2", [128, 2048], F32)
    CS = sb("CSt", [128, 2 * NT * 8], F32)
    MU = sb("MUt", [128, 2 * NCH], F32)
    IDB = sb("IDB", [128, 128], BF16)
    IDF = sb("IDF", [128, 128], F32)
    SSQ = sb("SSQ", [128, 24], F32)
    MEAN = sb("MEAN", [128, 4], F32)
    EPSC = sb("EPSC", [128, 1], F32)
    SDEV = sb("SDEV", [128, 2], F32)
    RSTD = sb("RSTD", [128, 2], F32)
    NEGMB = sb("NEGMB", [128, 2], F32)
    RTMP = sb("RTMP", [128, 256], F32)
    NQT = sb("NQT", [128, 16], F32)
    NQA = sb("NQA", [128, 16], F32)
    NKA = sb("NKA", [128, 16], F32)
    CROW = sb("CROW", [1, 32], F32)
    NEGC = sb("NEGC", [128, 1], F32)
    PP = sb("PP", [128, 4096], BF16)
    RDBn = sb("RDBn", [128, 1024], F32)
    DENR = sb("DENR", [128, 512], F32)
    DENC = sb("DENC", [128, 512], F32)
    ONESB = sb("ONESB", [128, 1], BF16)
    WSF = sb("WSF", [128, 1024], F32)
    WSB = sb("WSB", [128, 1024], BF16)
    FK = WSF[:, 0:512]

    sx = stack.enter_context(nc.semaphore("sx"))
    sg = stack.enter_context(nc.semaphore("sg"))
    st_ = stack.enter_context(nc.semaphore("st"))
    sv = stack.enter_context(nc.semaphore("sv"))
    ss = stack.enter_context(nc.semaphore("ss"))
    scw = stack.enter_context(nc.semaphore("scw"))
    sldA = stack.enter_context(nc.semaphore("sldA"))
    sldB = stack.enter_context(nc.semaphore("sldB"))
    SEM = dict(x=sx, g=sg, t=st_, v=sv, s=ss)
    C = dict(x=0, g=0, t=0, v=0, s=0)

    def inc(key, instr, k=1):
        instr.then_inc(SEM[key], k)
        C[key] += k
        return C[key]

    CANARY_N = [0]

    def emit_canary(waits=()):
        idx = CANARY_N[0]
        CANARY_N[0] += 1

        @blk.gpsimd
        def _(gpsimd):
            for (sem, val) in waits:
                gpsimd.wait_ge(sem, val)
            inc("g", gpsimd.dma_start(out=canary[0:1, idx:idx + 1],
                                      in_=MU[0:1, 0:1]), 16)

    blk = stack.enter_context(nc.Block())

    HTv = G1[:].rearrange("p (c t) -> p c t", c=NCH)
    Qv = G1[:].rearrange("p (c t) -> p c t", c=NCH)
    QTv = G2[:, 0:16384].rearrange("p (i n) -> p i n", i=NT)
    KTv = QTv
    VVv = G2[:, 0:16384].rearrange("p (i n) -> p i n", i=NT)
    H2Tv = G2[:, 0:16384].rearrange("p (c t) -> p c t", c=NCH)
    Kv = G3[:].rearrange("p (c t) -> p c t", c=NCH)
    latKv = G6[:, 0:4096].rearrange("p (l t) -> p l t", l=2)
    latVv = G6[:, 4096:8192].rearrange("p (l t) -> p l t", l=2)
    woTv = G6[:].rearrange("p (c n) -> p c n", c=NCH)
    wqf = G5[:, 0:8192].rearrange("p (c n) -> p c n", c=NCH)
    wkas = G5[:, 8192:10240].rearrange("p (c n) -> p c n", c=NCH)
    wkbs = G5[:, 10240:12288].rearrange("p (l n) -> p l n", l=2)
    wvas = G5[:, 12288:14336].rearrange("p (c n) -> p c n", c=NCH)
    wvbs = G5[:, 14336:16384].rearrange("p (l n) -> p l n", l=2)
    Oallv = G5[:].rearrange("p (c n) -> p c n", c=NCH)
    XRv = G5[:, 0:4096].rearrange("p (c n) -> p c n", c=NCH)
    XKv = G5[:, 4096:8192].rearrange("p (c n) -> p c n", c=NCH)
    Rv = G5[:, 8192:12288].rearrange("p (i n) -> p i n", i=4)
    WrTv = G1[:, 0:8192].rearrange("p (c n) -> p c n", c=NCH)
    kactv = G1[:, 8192:12288].rearrange("p (c n) -> p c n", c=NCH)
    WkTv = G3[:, 0:8192].rearrange("p (c n) -> p c n", c=NCH)
    WvTv = G3[:, 8192:16384].rearrange("p (c n) -> p c n", c=NCH)
    cosv = CS[:, 0:NT * 8].rearrange("p (i f) -> p i f", i=NT)
    sinv = CS[:, NT * 8:].rearrange("p (i f) -> p i f", i=NT)
    G7v = G7[:].rearrange("p (k n) -> p k n", k=2)

    @blk.sync
    def _(sync):
        inc("x", sync.dma_start(out=IDB[:], in_=idb_in), 16)
        inc("x", sync.dma_start(out=IDF[:], in_=idf_in), 16)
        inc("x", sync.dma_start(out=CS[:, 0:NT * 8], in_=cos_in), 16)
        inc("x", sync.dma_start(out=CS[:, NT * 8:], in_=sin_in), 16)
        inc("x", sync.dma_start(out=MU[:, 0:NCH], in_=mur_in), 16)
        inc("x", sync.dma_start(out=MU[:, NCH:], in_=muk_in), 16)

    _wdims = {n: (r, c) for (n, r, c) in _WSPECS}

    def _chunks_of(names):
        out = []
        for n in names:
            r, c = _wdims[n]
            for p in range(r // 128):
                for h in range(0, c, 512):
                    out.append((n, p, h, min(512, c - h)))
        return out

    chunks = (_chunks_of(["wkaT", "wvaT", "wkbT", "wvbT", "wqT"])
              + _chunks_of(["woT", "wrT", "wkT2", "wvT2"]))
    NCK = len(chunks)
    N_WEAVE = NCK
    SCW_SMALLW = 24
    SCW_WQ = 40

    def _direct_dst(n, p, h, w):
        if n == "wkaT":
            return wkas[:, p, 0:w]
        if n == "wvaT":
            return wvas[:, p, 0:w]
        if n == "wkbT":
            return wkbs[:, p, h:h + w]
        if n == "wvbT":
            return wvbs[:, p, h:h + w]
        if n == "wqT":
            return wqf[:, p, h:h + w]
        return None

    bounce_idx = {}
    for j, (n, p, h, w) in enumerate(chunks):
        if n in ("woT", "wrT", "wkT2", "wvT2"):
            bounce_idx[j] = len(bounce_idx)
    g_store, P0g = {}, {}

    @blk.gpsimd
    def _(gpsimd):
        for j, (n, p, h, w) in enumerate(chunks):
            if j not in bounce_idx:
                continue
            bj = bounce_idx[j]
            gpsimd.wait_ge(scw, j + 1)
            if bj >= 1:
                gpsimd.wait_ge(sg, g_store[bj - 1])
            s2 = (bj % 2) * 512
            inc("g", gpsimd.dma_start(
                out=w_bf[n][p * 128:(p + 1) * 128, h:h + w],
                in_=WSB[:, s2:s2 + w]), 16)
            g_store[bj] = C["g"]
            if j + 1 == NCK or chunks[j + 1][0] != n:
                P0g[n] = C["g"]
    P0_done_g = C["g"]

    CAST_J = [0]
    LD_J = [0]

    def emit_casts(scalar, budget, limit=N_WEAVE):
        while CAST_J[0] < limit and budget > 0:
            j = CAST_J[0]
            while LD_J[0] < min(j + 2, limit):
                i = LD_J[0]
                n, p, h, w = chunks[i]
                if i >= 2:
                    scalar.wait_ge(scw, i - 1)
                s = (i % 2) * 512
                scalar.dma_start(out=WSF[:, s:s + w],
                                 in_=w_in[n][p * 128:(p + 1) * 128, h:h + w]
                                 ).then_inc(sldA if i % 2 == 0 else sldB, 16)
                LD_J[0] += 1
            n, p, h, w = chunks[j]
            scalar.wait_ge(sldA if j % 2 == 0 else sldB, 16 * (j // 2 + 1))
            dst = _direct_dst(n, p, h, w)
            if dst is None:
                bj = bounce_idx[j]
                if bj >= 2:
                    scalar.wait_ge(sg, g_store[bj - 2])
                dst = WSB[:, (bj % 2) * 512:(bj % 2) * 512 + w]
            s = (j % 2) * 512
            scalar.activation(dst, WSF[:, s:s + w],
                              AF.Copy).then_inc(scw, 1)
            CAST_J[0] += 1
            budget -= 1
    emit_canary([(sg, P0_done_g), (sx, 16 * 6)])

    def emit_ln_phase(src_hbm, dstv, ps_t, sync_waits, vec_waits,
                      scalar_extra=None):
        X0, V0, S0, T0 = C["x"], C["v"], C["s"], C["t"]

        @blk.sync
        def _(sync):
            for (sem, val) in sync_waits:
                sync.wait_ge(sem, val)
            for i in range(NT):
                if i >= 2:
                    sync.wait_ge(sv, V0 + 5 * (i - 2) + 1)
                    sync.wait_ge(ss, S0 + 2 * (i - 2) + 2)
                s = i % 2
                inc("x", sync.dma_start(out=F1[:, s * 1024:(s + 1) * 1024],
                                        in_=src_hbm[i * 128:(i + 1) * 128, :]), 16)

        @blk.vector
        def _(vector):
            for (sem, val) in vec_waits:
                vector.wait_ge(sem, val)
            for i in range(NT):
                s = i % 2
                vector.wait_ge(sx, X0 + 16 * (i + 1))
                vector.bn_stats(SSQ[:, 12 * s:12 * s + 6],
                                F1[:, s * 1024:s * 1024 + 512])
                inc("v", vector.bn_stats(SSQ[:, 12 * s + 6:12 * s + 12],
                                         F1[:, s * 1024 + 512:(s + 1) * 1024]))
                vector.wait_ge(sv, V0 + 5 * i + 1)
                if i >= 2:
                    vector.wait_ge(ss, S0 + 2 * (i - 2) + 1)
                inc("v", vector.bn_aggr(MEAN[:, 2 * s:2 * s + 2],
                                        SSQ[:, 12 * s:12 * s + 12]))
                vector.wait_ge(ss, S0 + 2 * i + 1)
                inc("v", vector.reciprocal(RSTD[:, s:s + 1], SDEV[:, s:s + 1]))
                vector.wait_ge(sv, V0 + 5 * i + 3)
                inc("v", vector.scalar_tensor_tensor(
                    NEGMB[:, s:s + 1], MEAN[:, 2 * s:2 * s + 1], -1.0,
                    RSTD[:, s:s + 1], ALU.mult, ALU.mult))
                vector.wait_ge(st_, T0 + i + 1)
                inc("v", vector.tensor_copy(
                    dstv[:, :, i * 128:(i + 1) * 128],
                    ps_t[:, s, :].rearrange("p (c n) -> p c n", c=NCH)))

        @blk.scalar
        def _(scalar):
            for i in range(NT):
                s = i % 2
                if scalar_extra:
                    scalar_extra(scalar, i)
                scalar.wait_ge(sv, V0 + 5 * i + 2)
                inc("s", scalar.activation(SDEV[:, s:s + 1],
                                           MEAN[:, 2 * s + 1:2 * s + 2],
                                           AF.Sqrt, bias=EPSC[:, 0:1]))
                scalar.wait_ge(sv, V0 + 5 * i + 4)
                if i >= 2:
                    scalar.wait_ge(st_, T0 + (i - 2) + 1)
                inc("s", scalar.activation(G7v[:, s, :],
                                           F1[:, s * 1024:(s + 1) * 1024],
                                           AF.Identity, bias=NEGMB[:, s:s + 1],
                                           scale=RSTD[:, s:s + 1]))

        @blk.tensor
        def _(tensor):
            for i in range(NT):
                s = i % 2
                tensor.wait_ge(ss, S0 + 2 * i + 2)
                if i >= 2:
                    tensor.wait_ge(sv, V0 + 5 * (i - 2) + 5)
                last = None
                for c in range(NCH):
                    last = tensor.transpose(
                        ps_t[:, s, c * 128:(c + 1) * 128],
                        G7v[:, s, c * 128:(c + 1) * 128], IDB[:])
                inc("t", last)
        assert C["v"] == V0 + 5 * NT and C["s"] == S0 + 2 * NT and C["t"] == T0 + NT

    ps_a_ctx = nc.psum_tensor("ps_a", [128, 1024], F32)
    ps_b_ctx = nc.psum_tensor("ps_b", [128, 1024], F32)
    ps_a = ps_a_ctx.__enter__()
    ps_b = ps_b_ctx.__enter__()
    ps_t_ctx = nc.psum_tensor("ps_t", [128, 2048], BF16)
    ps_t = ps_t_ctx.__enter__()
    ps_c_ctx = nc.psum_tensor("ps_c", [128, 1024], F32)
    ps_c = ps_c_ctx.__enter__()
    PS3 = [ps_a, ps_b, ps_c]
    ps_tv = ps_t[:].rearrange("p (k n) -> p k n", k=2)

    @blk.vector
    def _(vector):
        inc("v", vector.memset(EPSC[:], EPS))

    ln1_v0 = C["v"]
    emit_ln_phase(x_in, HTv, ps_tv, sync_waits=[], vec_waits=[],
                  scalar_extra=lambda scalar, i: emit_casts(scalar, 3))
    emit_canary([(sv, C["v"])])

    T0, S0 = C["t"], C["s"]
    fills = [(w, lc, tb) for tb in range(4) for w in range(2) for lc in range(2)]
    HT_ready_v = C["v"]

    @blk.tensor
    def _(tensor):
        tensor.wait_ge(scw, 16)
        for j, (which, lc, tb) in enumerate(fills):
            pss = PS3[j % 2][:, 0:512]
            if j % 4 == 0:
                tensor.wait_ge(sv, ln1_v0 + 20 * (tb + 1))
            if j >= 2:
                tensor.wait_ge(ss, S0 + j - 1)
            wsl = wkas if which == 0 else wvas
            last = None
            for c in range(NCH):
                last = tensor.matmul(pss, wsl[:, c, lc * 128:(lc + 1) * 128],
                                     HTv[:, c, tb * 512:(tb + 1) * 512],
                                     start=(c == 0), stop=(c == NCH - 1))
            inc("t", last)

    @blk.scalar
    def _(scalar):
        for j, (which, lc, tb) in enumerate(fills):
            pss = PS3[j % 2][:, 0:512]
            emit_casts(scalar, 1)
            scalar.wait_ge(st_, T0 + j + 1)
            dst = (latKv if which == 0 else latVv)[:, lc, tb * 512:(tb + 1) * 512]
            inc("s", scalar.activation(dst, pss, AF.Copy))
    assert C["t"] == T0 + 16 and C["s"] == S0 + 16
    emit_canary([(ss, C["s"])])

    def emit_qk_pipeline(src_is_q, dstv_feat, acc):
        nacc = NCH if src_is_q else 2
        T0b, S0b, V0b = C["t"], C["s"], C["v"]
        t_proj, t_trans = {}, {}
        tc0 = T0b
        for i in range(NT):
            for half in range(2):
                tc0 += 1
                t_proj[(i, half)] = tc0
            if i >= 2:
                tc0 += 1
                t_trans[i - 2] = tc0
        for ii in (NT - 2, NT - 1):
            tc0 += 1
            t_trans[ii] = tc0
        s_copy, s_square = {}, {}
        sc0 = S0b
        for i in range(NT):
            for half in range(2):
                sc0 += 1
                s_copy[(i, half)] = sc0
            if i >= 1:
                sc0 += 1
                s_square[i - 1] = sc0
        sc0 += 1
        s_square[NT - 1] = sc0
        v_rope, v_transcopy, v_normred = {}, {}, {}
        vc0 = V0b
        for i in range(NT):
            for half in range(2):
                vc0 += 6
                v_rope[(i, half)] = vc0
            if i >= 2:
                vc0 += 1
                v_transcopy[i - 2] = vc0
            if i >= 1:
                vc0 += 2
                v_normred[i - 1] = vc0
        for ii in (NT - 2, NT - 1):
            vc0 += 1
            v_transcopy[ii] = vc0
        vc0 += 2
        v_normred[NT - 1] = vc0

        def trans_unit(tensor, ii):
            tensor.wait_ge(sv, v_rope[(ii, 1)])
            if ii >= 2:
                tensor.wait_ge(sv, v_transcopy[ii - 2])
            s = ii % 2
            last = None
            for c in range(NCH):
                last = tensor.transpose(ps_tv[:, s, c * 128:(c + 1) * 128],
                                        QTv[:, ii, c * 128:(c + 1) * 128], IDB[:])
            inc("t", last)
            assert C["t"] == t_trans[ii]

        @blk.tensor
        def _(tensor):
            tensor.wait_ge(scw, SCW_WQ if src_is_q else SCW_SMALLW)
            for i in range(NT):
                for half in range(2):
                    j = 2 * i + half
                    pss = PS3[j % 3][:, 0:512]
                    if j >= 3:
                        jp = j - 3
                        tensor.wait_ge(ss, s_copy[(jp // 2, jp % 2)])
                        tensor.wait_ge(sv, v_rope[(jp // 2, jp % 2)])
                    else:
                        tensor.wait_ge(ss, S0b)
                        tensor.wait_ge(sv, V0b)
                    last = None
                    for a in range(nacc):
                        if src_is_q:
                            lhs = HTv[:, a, i * 128:(i + 1) * 128]
                            rhs = wqf[:, a, half * 512:(half + 1) * 512]
                        else:
                            lhs = latKv[:, a, i * 128:(i + 1) * 128]
                            rhs = wkbs[:, a, half * 512:(half + 1) * 512]
                        last = tensor.matmul(pss, lhs, rhs,
                                             start=(a == 0), stop=(a == nacc - 1))
                    inc("t", last)
                    assert C["t"] == t_proj[(i, half)]
                if i >= 2:
                    trans_unit(tensor, i - 2)
            for ii in (NT - 2, NT - 1):
                trans_unit(tensor, ii)

        def square_unit(scalar, ii):
            scalar.wait_ge(sv, v_rope[(ii, 1)])
            if ii >= 2:
                scalar.wait_ge(sv, v_normred[ii - 2])
            inc("s", scalar.activation(
                F2[:, (ii % 2) * 1024:(ii % 2 + 1) * 1024],
                QTv[:, ii, :], AF.Square))
            assert C["s"] == s_square[ii]

        @blk.scalar
        def _(scalar):
            for i in range(NT):
                for half in range(2):
                    pss = PS3[(2 * i + half) % 3][:, 0:512]
                    emit_casts(scalar, 1)
                    scalar.wait_ge(st_, t_proj[(i, half)])
                    inc("s", scalar.activation(
                        QTv[:, i, half * 512:(half + 1) * 512], pss, AF.Copy))
                    assert C["s"] == s_copy[(i, half)]
                if i >= 1:
                    square_unit(scalar, i - 1)
            square_unit(scalar, NT - 1)

        def transcopy_unit(vector, ii):
            vector.wait_ge(st_, t_trans[ii])
            inc("v", vector.tensor_copy(
                dstv_feat[:, :, ii * 128:(ii + 1) * 128],
                ps_tv[:, ii % 2, :].rearrange("p (c n) -> p c n", c=NCH)))
            assert C["v"] == v_transcopy[ii]

        def normred_unit(vector, ii):
            vector.wait_ge(ss, s_square[ii])
            inc("v", vector.reduce_sum(
                NQT[:].rearrange("p (h u) -> p h u", h=16, u=1),
                F2[:, (ii % 2) * 1024:(ii % 2 + 1) * 1024]
                .rearrange("p (h u) -> p h u", h=16), axis=AXX))
            vector.wait_ge(sv, C["v"])
            if ii == 0:
                inc("v", vector.tensor_copy(acc[:], NQT[:]))
            else:
                inc("v", vector.tensor_tensor(acc[:], acc[:], NQT[:], ALU.max))
            assert C["v"] == v_normred[ii]

        @blk.vector
        def _(vector):
            for i in range(NT):
                for half in range(2):
                    j = 2 * i + half
                    ps_full = PS3[j % 3]
                    vector.wait_ge(ss, s_copy[(i, half)])
                    ap = ps_full[:, 0:512].rearrange("p (h a b) -> p h a b", h=8, b=2)
                    xa = ap[:, :, 0:8, 0:1]
                    xb = ap[:, :, 0:8, 1:2]
                    cv = cosv[:, half * 8:(half + 1) * 8, :].unsqueeze(3)
                    sn = sinv[:, half * 8:(half + 1) * 8, :].unsqueeze(3)
                    ta = RTMP[:, 0:64].rearrange("p (h a b) -> p h a b", h=8, a=8)
                    tb = RTMP[:, 64:128].rearrange("p (h a b) -> p h a b", h=8, a=8)
                    tg = RTMP[:, 128:192].rearrange("p (h a b) -> p h a b", h=8, a=8)
                    td = RTMP[:, 192:256].rearrange("p (h a b) -> p h a b", h=8, a=8)
                    out = QTv[:, i, half * 512:(half + 1) * 512] \
                        .rearrange("p (h a b) -> p h a b", h=8, b=2)
                    oe = out[:, :, 0:8, 0:1]
                    oo = out[:, :, 0:8, 1:2]
                    vector.tensor_tensor(ta, xa, cv, ALU.mult)
                    vector.tensor_tensor(tb, xb, sn, ALU.mult)
                    vector.tensor_tensor(tg, xa, sn, ALU.mult)
                    inc("v", vector.tensor_tensor(td, xb, cv, ALU.mult))
                    vector.wait_ge(sv, C["v"])
                    vector.tensor_tensor(oe, ta, tb, ALU.subtract)
                    inc("v", vector.tensor_tensor(oo, tg, td, ALU.add), 5)
                    assert C["v"] == v_rope[(i, half)]
                if i >= 2:
                    transcopy_unit(vector, i - 2)
                if i >= 1:
                    normred_unit(vector, i - 1)
            for ii in (NT - 2, NT - 1):
                transcopy_unit(vector, ii)
            normred_unit(vector, NT - 1)
        assert C["t"] == tc0 and C["s"] == sc0 and C["v"] == vc0

    emit_qk_pipeline(True, Qv, NQA)
    emit_canary([(sv, C["v"]), (ss, C["s"])])
    emit_qk_pipeline(False, Kv, NKA)
    K_done_v = C["v"]

    T0v, V0v, S0v = C["t"], C["v"], C["s"]

    @blk.tensor
    def _(tensor):
        tensor.wait_ge(sv, K_done_v)
        tensor.wait_ge(ss, S0v)
        tensor.wait_ge(scw, SCW_SMALLW)
        for i in range(NT):
            for half in range(2):
                j = i * 2 + half
                pss = PS3[j % 2][:, 0:512]
                if j >= 2:
                    tensor.wait_ge(sv, V0v + 1 + (j - 1))
                last = None
                for a in range(2):
                    last = tensor.matmul(pss, latVv[:, a, i * 128:(i + 1) * 128],
                                         wvbs[:, a, half * 512:(half + 1) * 512],
                                         start=(a == 0), stop=(a == 1))
                inc("t", last)

    @blk.vector
    def _(vector):
        vector.wait_ge(sv, K_done_v)
        inc("v", vector.memset(ONESB[:], 1.0))
        for i in range(NT):
            for half in range(2):
                j = i * 2 + half
                pss = PS3[j % 2][:, 0:512]
                vector.wait_ge(st_, T0v + j + 1)
                inc("v", vector.tensor_copy(
                    VVv[:, i, half * 512:(half + 1) * 512], pss))
    VV_done_v = C["v"]
    VV_fill_done_t = C["t"]
    assert C["t"] == T0v + 32 and C["v"] == V0v + 33
    emit_canary([(sv, C["v"])])

    T0n, V0n, S0n = C["t"], C["v"], C["s"]

    @blk.tensor
    def _(tensor):
        tensor.wait_ge(sv, VV_done_v)
        inc("t", tensor.transpose(ps_a[0:16, 0:128], NQA[:], IDF[:]))
        inc("t", tensor.transpose(ps_b[0:16, 0:128], NKA[:], IDF[:]))

    @blk.vector
    def _(vector):
        vector.wait_ge(st_, T0n + 2)
        vector.reduce_max(NQT[0:16, 0:1], ps_a[0:16, 0:128], axis=AXX)
        inc("v", vector.reduce_max(NQT[0:16, 1:2], ps_b[0:16, 0:128], axis=AXX))
        vector.wait_ge(sv, C["v"])
        inc("v", vector.tensor_tensor(NQT[0:16, 2:3], NQT[0:16, 0:1],
                                      NQT[0:16, 1:2], ALU.mult))

    @blk.scalar
    def _(scalar):
        scalar.wait_ge(sv, V0n + 2)
        inc("s", scalar.activation(NQT[0:16, 3:4], NQT[0:16, 2:3], AF.Sqrt))

    @blk.vector
    def _(vector):
        vector.wait_ge(ss, S0n + 1)
        inc("v", vector.tensor_scalar(NQT[0:16, 4:5], NQT[0:16, 3:4],
                                      0.125, None, ALU.mult))

    @blk.tensor
    def _(tensor):
        tensor.wait_ge(sv, V0n + 3)
        inc("t", tensor.transpose(ps_a[0:1, 0:16], NQT[0:16, 4:5], IDF[0:16, 0:16]))

    @blk.vector
    def _(vector):
        vector.wait_ge(st_, T0n + 3)
        inc("v", vector.tensor_copy(CROW[0:1, 0:16], ps_a[0:1, 0:16]))
        vector.wait_ge(sv, C["v"])
        inc("v", vector.reduce_max(CROW[0:1, 16:17], CROW[0:1, 0:16], axis=AXX))
        vector.wait_ge(sv, C["v"])
        inc("v", vector.tensor_scalar(CROW[0:1, 17:18], CROW[0:1, 16:17],
                                      -60.0, 0.0, ALU.add, ALU.max))
        vector.wait_ge(sv, C["v"])
        inc("v", vector.tensor_scalar(CROW[0:1, 18:19], CROW[0:1, 17:18],
                                      -1.0, None, ALU.mult))

    @blk.gpsimd
    def _(gpsimd):
        gpsimd.wait_ge(sv, V0n + 7)
        inc("g", gpsimd.dma_start(
            out=NEGC[:], in_=CROW[0:1, 18:19].unsqueeze(1)
            .broadcast_to([1, 128, 1])), 16)
    negc_g = C["g"]
    emit_canary([(sg, negc_g)])

    @blk.scalar
    def _(scalar):
        emit_casts(scalar, NCK)
    assert CAST_J[0] == NCK

    ps_c_ctx.__exit__(None, None, None)
    ps_t_ctx.__exit__(None, None, None)
    ps_b_ctx.__exit__(None, None, None)
    ps_a_ctx.__exit__(None, None, None)

    psS_ctx = nc.psum_tensor("psS", [128, 2048], F32)
    psAV_ctx = nc.psum_tensor("psAV", [128, 1024], F32)
    psDN_ctx = nc.psum_tensor("psDN", [128, 512], F32)
    psWO_ctx = nc.psum_tensor("psWO", [128, 512], F32)
    psS = psS_ctx.__enter__()
    psAV = psAV_ctx.__enter__()
    psDN = psDN_ctx.__enter__()
    psWO = psWO_ctx.__enter__()

    NU = NQB * NPAIR * 32

    def uidx(u):
        G, s = divmod(u, 32)
        kt, o = divmod(s, 2)
        return G // NPAIR, G % NPAIR, kt, o

    wo_groups = [(qb, tt, o) for qb in range(NQB) for tt in range(8)
                 for o in range(2)]
    wo_before = {}
    for k in range(16):
        wo_before.setdefault(264 + 16 * k, []).append(k)
    wo_tail = list(range(16, 32))

    def wo_bank(k):
        if k < 16:
            return psWO
        return psWO if (k - 16) % 2 == 0 else psDN

    events = []
    for it in range(NU + 2):
        for k in wo_before.get(it, []):
            events.append(("wo", k))
        if it < NU:
            events.append(("S", it))
        if it >= 2:
            events.append(("dnav", it - 2))
    for k in wo_tail:
        events.append(("wo", k))

    t_S, t_dnav, t_wo = {}, {}, {}
    tc = C["t"]
    for ev in events:
        tc += 1
        if ev[0] == "S":
            t_S[ev[1]] = tc
        elif ev[0] == "dnav":
            t_dnav[ev[1]] = tc
        else:
            t_wo[ev[1]] = tc

    S0p = C["s"]
    s_exp = {u: S0p + u + 1 for u in range(NU)}

    vec_events = []
    evac_in_G = {G: [] for G in range(2 * NPAIR)}
    for k in range(16):
        evac_in_G[min((264 + 16 * k) // 32, 2 * NPAIR - 1)].append(k)
    for G in range(2 * NPAIR):
        for k in evac_in_G[G]:
            vec_events.append(("x1", k))
        vec_events.append(("evac", G))
        vec_events.append(("dncp", G))
        vec_events.append(("recip", G))
        vec_events.append(("norm", G))
    for k in wo_tail:
        vec_events.append(("x1", k))

    v_evac, v_dncp, v_recip, v_norm, v_x1 = {}, {}, {}, {}, {}
    vc = C["v"]
    for ev in vec_events:
        vc += 1
        if ev[0] == "evac":
            v_evac[ev[1]] = vc
        elif ev[0] == "dncp":
            v_dncp[ev[1]] = vc
        elif ev[0] == "recip":
            v_recip[ev[1]] = vc
        elif ev[0] == "norm":
            v_norm[ev[1]] = vc
        else:
            v_x1[ev[1]] = vc

    g_rdb = {}
    gc = C["g"]
    for G in range(2 * NPAIR):
        gc += 64
        g_rdb[G] = gc

    x_wot = C["x"] + 16
    x_rw, x_st = {}, {}
    xc = x_wot
    for qb in range(NQB):
        for tt in range(8):
            xc += 16
            x_rw[(qb, tt)] = xc
            if tt >= 1:
                xc += 16
                x_st[(qb, tt - 1)] = xc
        xc += 16
        x_st[(qb, 7)] = xc

    @blk.tensor
    def _(tensor):
        for ev in events:
            if ev[0] == "S":
                u = ev[1]
                qb, p, kt, o = uidx(u)
                if u >= 2:
                    tensor.wait_ge(ss, s_exp[u - 2])
                else:
                    tensor.wait_ge(sv, VV_done_v)
                    tensor.wait_ge(sg, negc_g)
                slot = (u % 2) * 1024
                q0 = qb * QBS + o * 512
                tensor.matmul(psS[:, slot:slot + 512],
                              Kv[0:64, p, kt * 128:(kt + 1) * 128],
                              Qv[0:64, p, q0:q0 + 512],
                              start=True, stop=True, tile_position=(0, 0))
                last = tensor.matmul(psS[:, slot + 512:slot + 1024],
                                     Kv[64:128, p, kt * 128:(kt + 1) * 128],
                                     Qv[64:128, p, q0:q0 + 512],
                                     start=True, stop=True, tile_position=(64, 0))
                inc("t", last)
                assert C["t"] == t_S[u]
            elif ev[0] == "dnav":
                u = ev[1]
                qb, p, kt, o = uidx(u)
                G = qb * NPAIR + p
                if u >= NU - 2:
                    tensor.wait_ge(ss, s_exp[u])
                if kt == 0 and o == 0 and G >= 1:
                    tensor.wait_ge(sv, v_dncp[G - 1])
                if kt == 0 and G >= 1:
                    tensor.wait_ge(sv, v_evac[G - 1])
                slot = (u % 4) * 1024
                pa = PP[:, slot:slot + 512]
                pb = PP[:, slot + 512:slot + 1024]
                rowA, rowB = 64 * o, 64 * o + 32
                st0, sp0 = (kt == 0), (kt == NKT - 1)
                tensor.matmul(psDN[rowA:rowA + 1, 0:512], ONESB[:, 0:1], pa,
                              start=st0, stop=sp0, tile_position=(0, rowA))
                tensor.matmul(psDN[rowB:rowB + 1, 0:512], ONESB[:, 0:1], pb,
                              start=st0, stop=sp0, tile_position=(0, rowB))
                tensor.matmul(psAV[0:64, o * 512:(o + 1) * 512],
                              VVv[:, kt, 128 * p:128 * p + 64], pa,
                              start=st0, stop=sp0, tile_position=(0, 0))
                last = tensor.matmul(psAV[64:128, o * 512:(o + 1) * 512],
                                     VVv[:, kt, 128 * p + 64:128 * p + 128], pb,
                                     start=st0, stop=sp0, tile_position=(0, 64))
                inc("t", last)
                assert C["t"] == t_dnav[u]
            else:
                k = ev[1]
                qb, tt, o = wo_groups[k]
                bank = wo_bank(k)
                if k == 0:
                    tensor.wait_ge(sx, x_wot)
                if k == 0 or k == 16:
                    tensor.wait_ge(sv, v_norm[qb * NPAIR + NPAIR - 1])
                elif k < 16:
                    tensor.wait_ge(sv, v_x1[k - 1])
                elif k >= 18:
                    tensor.wait_ge(sv, v_x1[k - 2])
                last = None
                for c in range(NCH):
                    last = tensor.matmul(
                        bank[:, :],
                        Oallv[:, c, qb * QBS + tt * 128:qb * QBS + (tt + 1) * 128],
                        woTv[:, c, o * 512:(o + 1) * 512],
                        start=(c == 0), stop=(c == NCH - 1))
                inc("t", last)
                assert C["t"] == t_wo[k]
    assert C["t"] == tc

    @blk.scalar
    def _(scalar):
        scalar.wait_ge(sg, negc_g)
        for u in range(NU):
            scalar.wait_ge(st_, t_S[u])
            pslot = (u % 4) * 1024
            sslot = (u % 2) * 1024
            inc("s", scalar.activation(PP[:, pslot:pslot + 1024],
                                       psS[:, sslot:sslot + 1024], AF.Exp,
                                       bias=NEGC[:, 0:1], scale=0.125))
            assert C["s"] == s_exp[u]

    @blk.vector
    def _(vector):
        for ev in vec_events:
            if ev[0] == "x1":
                k = ev[1]
                qb, tt, o = wo_groups[k]
                bank = wo_bank(k)
                vector.wait_ge(st_, t_wo[k])
                vector.wait_ge(sx, x_rw[(qb, tt)])
                if o == 0 and not (qb == 0 and tt < 2):
                    pq, ptt = (qb, tt - 2) if tt >= 2 else (qb - 1, tt + 6)
                    vector.wait_ge(sx, x_st[(pq, ptt)])
                half = (tt % 2) * 1024 + o * 512
                inc("v", vector.tensor_tensor(
                    F2[:, half:half + 512], bank[:, :],
                    F1[:, half:half + 512], ALU.add))
                assert C["v"] == v_x1[k]
            elif ev[0] == "evac":
                G = ev[1]
                qb, p = divmod(G, NPAIR)
                vector.wait_ge(st_, t_dnav[G * 32 + 31])
                inc("v", vector.tensor_copy(
                    Oallv[:, p, qb * QBS:(qb + 1) * QBS], psAV[:, 0:1024]))
                assert C["v"] == v_evac[G]
            elif ev[0] == "dncp":
                G = ev[1]
                inc("v", vector.tensor_copy(DENC[0:97, 0:512],
                                            psDN[0:97, 0:512]))
                assert C["v"] == v_dncp[G]
            elif ev[0] == "recip":
                G = ev[1]
                vector.wait_ge(sv, v_dncp[G])
                if G >= 1:
                    vector.wait_ge(sg, g_rdb[G - 1])
                inc("v", vector.reciprocal(DENR[0:97, 0:512],
                                           DENC[0:97, 0:512]))
                assert C["v"] == v_recip[G]
            else:
                G = ev[1]
                qb, p = divmod(G, NPAIR)
                vector.wait_ge(sg, g_rdb[G])
                osl = Oallv[:, p, qb * QBS:(qb + 1) * QBS]
                inc("v", vector.tensor_tensor(osl, osl, RDBn[:, 0:1024],
                                              ALU.mult))
                assert C["v"] == v_norm[G]
    assert C["v"] == vc

    @blk.gpsimd
    def _(gpsimd):
        for G in range(2 * NPAIR):
            gpsimd.wait_ge(sv, v_recip[G])
            if G >= 1:
                gpsimd.wait_ge(sv, v_norm[G - 1])
            for (row, p0, c0) in [(0, 0, 0), (64, 0, 512),
                                  (32, 64, 0), (96, 64, 512)]:
                inc("g", gpsimd.dma_start(
                    out=RDBn[p0:p0 + 64, c0:c0 + 512],
                    in_=DENR[row:row + 1, 0:512].unsqueeze(1)
                    .broadcast_to([1, 64, 512])), 16)
            assert C["g"] == g_rdb[G]
    assert C["g"] == gc

    @blk.sync
    def _(sync):
        sync.wait_ge(sg, P0g["woT"])
        sync.wait_ge(st_, VV_fill_done_t)
        inc("x", sync.dma_start(out=woTv[:, :, :],
                                in_=w_bf["woT"].rearrange("(c p) n -> p c n", p=128)), 16)
        assert C["x"] == x_wot
        for qb in range(NQB):
            for tt in range(8):
                s2 = tt % 2
                if qb == 0 and tt < 2:
                    sync.wait_ge(sv, VV_done_v)
                else:
                    pq, ptt = (qb, tt - 2) if tt >= 2 else (qb - 1, tt + 6)
                    sync.wait_ge(sv, v_x1[pq * 16 + ptt * 2 + 1])
                inc("x", sync.dma_start(
                    out=F1[:, s2 * 1024:(s2 + 1) * 1024],
                    in_=x_in[qb * QBS + tt * 128: qb * QBS + (tt + 1) * 128, :]), 16)
                assert C["x"] == x_rw[(qb, tt)]
                if tt >= 1:
                    sync.wait_ge(sv, v_x1[qb * 16 + (tt - 1) * 2 + 1])
                    inc("x", sync.dma_start(
                        out=x1_sp[qb * QBS + (tt - 1) * 128: qb * QBS + tt * 128, :],
                        in_=F2[:, ((tt - 1) % 2) * 1024:((tt - 1) % 2 + 1) * 1024]), 16)
                    assert C["x"] == x_st[(qb, tt - 1)]
            sync.wait_ge(sv, v_x1[qb * 16 + 7 * 2 + 1])
            inc("x", sync.dma_start(
                out=x1_sp[qb * QBS + 7 * 128: qb * QBS + 8 * 128, :],
                in_=F2[:, 1024:2048]), 16)
            assert C["x"] == x_st[(qb, 7)]
    assert C["x"] == xc

    psWO_ctx.__exit__(None, None, None)
    psDN_ctx.__exit__(None, None, None)
    psAV_ctx.__exit__(None, None, None)
    psS_ctx.__exit__(None, None, None)
    emit_canary([(sx, C["x"]), (sv, C["v"]), (ss, C["s"]), (st_, C["t"])])

    ps_t2_ctx = nc.psum_tensor("ps_t2", [128, 2048], BF16)
    ps_t2 = ps_t2_ctx.__enter__()
    ps_t2v = ps_t2[:].rearrange("p (k n) -> p k n", k=2)
    emit_ln_phase(x1_sp, H2Tv, ps_t2v,
                  sync_waits=[(sx, C["x"]), (sv, C["v"])],
                  vec_waits=[])
    ps_t2_ctx.__exit__(None, None, None)
    emit_canary([(sv, C["v"])])

    ps_a_ctx2 = nc.psum_tensor("ps_a2", [128, 1024], F32)
    ps_b_ctx2 = nc.psum_tensor("ps_b2", [128, 1024], F32)
    ps_e_ctx = nc.psum_tensor("ps_e", [128, 1024], F32)
    ps_f_ctx = nc.psum_tensor("ps_f", [128, 1024], F32)
    ps_a2 = ps_a_ctx2.__enter__()
    ps_b2 = ps_b_ctx2.__enter__()
    ps_e = ps_e_ctx.__enter__()
    ps_f = ps_f_ctx.__enter__()

    post_p4a_v = C["v"]

    @blk.sync
    def _(sync):
        sync.wait_ge(sv, post_p4a_v)
        sync.wait_ge(sg, P0_done_g)
        inc("x", sync.dma_start(out=WrTv[:, :, :],
                       in_=w_bf["wrT"].rearrange("(c p) n -> p c n", p=128)), 16)
        inc("x", sync.dma_start(out=WkTv[:, :, :],
                       in_=w_bf["wkT2"].rearrange("(c p) n -> p c n", p=128)), 16)
        inc("x", sync.dma_start(out=WvTv[:, :, :],
                                in_=w_bf["wvT2"].rearrange("(c p) n -> p c n", p=128)), 16)
    ffnw_x = C["x"]

    NSL = 4
    T0f, S0f, V0f, X0f = C["t"], C["s"], C["v"], C["x"]
    t_r, t_k, t_o = {}, {}, {}
    tc2 = T0f
    for s in range(NSL):
        for tt in range(4):
            tc2 += 1
            t_r[(s, tt)] = tc2
        for co in range(NCH):
            tc2 += 1
            t_k[(s, co)] = tc2
        for tt in range(4):
            tc2 += 1
            t_o[(s, tt)] = tc2
    s_r, s_k = {}, {}
    sc2 = S0f
    for s in range(NSL):
        for tt in range(4):
            sc2 += 1
            s_r[(s, tt)] = sc2
        for co in range(NCH):
            sc2 += 2
            s_k[(s, co)] = sc2
    v_xr, v_o = {}, {}
    vc2 = V0f
    for s in range(NSL):
        vc2 += 1 + 2 * NCH + (2 if s == 0 else 0)
        v_xr[s] = vc2
        for tt in range(4):
            vc2 += 1
            v_o[(s, tt)] = vc2
    x_x1, x_y = {}, {}
    xc2 = X0f
    for s in range(NSL):
        for tt in range(4):
            xc2 += 16
            x_x1[(s, tt)] = xc2
            if tt >= 1:
                xc2 += 16
                x_y[(s, tt - 1)] = xc2
        xc2 += 16
        x_y[(s, 3)] = xc2

    @blk.vector
    def _(vector):
        for s in range(NSL):
            if s >= 1:
                vector.wait_ge(st_, t_k[(s - 1, NCH - 1)])
            base = s * 512
            lo = 1 if s == 0 else 0
            if s == 0:
                inc("v", vector.tensor_tensor(
                    XKv[:, :, 1:512], H2Tv[:, :, 1:512], H2Tv[:, :, 0:511],
                    ALU.subtract))
            else:
                inc("v", vector.tensor_tensor(
                    XKv[:, :, 0:512], H2Tv[:, :, base:base + 512],
                    H2Tv[:, :, base - 1:base + 511], ALU.subtract))
            for c in range(NCH):
                sh = H2Tv[:, c, base + lo - 1:base + 511]
                inc("v", vector.scalar_tensor_tensor(
                    XRv[:, c, lo:512], XKv[:, c, lo:512], MU[:, c:c + 1], sh,
                    ALU.mult, ALU.add))
            for c in range(NCH):
                sh = H2Tv[:, c, base + lo - 1:base + 511]
                inc("v", vector.scalar_tensor_tensor(
                    XKv[:, c, lo:512], XKv[:, c, lo:512], MU[:, NCH + c:NCH + c + 1],
                    sh, ALU.mult, ALU.add))
            if s == 0:
                last = None
                for c in range(NCH):
                    vector.tensor_scalar(XRv[:, c, 0:1], H2Tv[:, c, 0:1],
                                         MU[:, c:c + 1], None, ALU.mult)
                    last = vector.tensor_scalar(XKv[:, c, 0:1], H2Tv[:, c, 0:1],
                                                MU[:, NCH + c:NCH + c + 1], None,
                                                ALU.mult)
                inc("v", last, 2)
            assert C["v"] == v_xr[s]
            for tt in range(4):
                vector.wait_ge(st_, t_o[(s, tt)])
                vector.wait_ge(sx, x_x1[(s, tt)])
                if tt >= 2:
                    vector.wait_ge(sx, x_y[(s, tt - 2)])
                s2 = tt % 2
                psl = ps_f if tt % 2 == 0 else ps_e
                vector.tensor_tensor(F2[:, s2 * 1024:(s2 + 1) * 1024],
                                     psl[:, :], Rv[:, tt, :], ALU.mult)
                inc("v", vector.tensor_tensor(
                    F2[:, s2 * 1024:(s2 + 1) * 1024],
                    F2[:, s2 * 1024:(s2 + 1) * 1024],
                    F1[:, s2 * 1024:(s2 + 1) * 1024], ALU.add))
                assert C["v"] == v_o[(s, tt)]
    assert C["v"] == vc2

    @blk.tensor
    def _(tensor):
        tensor.wait_ge(sx, ffnw_x)
        for s in range(NSL):
            tensor.wait_ge(sv, v_xr[s])
            for tt in range(4):
                sl = ps_a2 if tt % 2 == 0 else ps_b2
                if not (s == 0 and tt < 2):
                    pq, ptt = (s, tt - 2) if tt >= 2 else (s - 1, tt + 2)
                    tensor.wait_ge(ss, s_r[(pq, ptt)])
                last = None
                for o in range(2):
                    for c in range(NCH):
                        last = tensor.matmul(
                            sl[:, o * 512:(o + 1) * 512],
                            XRv[:, c, tt * 128:(tt + 1) * 128],
                            WrTv[:, c, o * 512:(o + 1) * 512],
                            start=(c == 0), stop=(c == NCH - 1))
                inc("t", last)
                assert C["t"] == t_r[(s, tt)]
            for co in range(NCH):
                sl = ps_e[:, 0:512] if co % 2 == 0 else ps_e[:, 512:1024]
                if co == 0 and s >= 1:
                    tensor.wait_ge(sv, v_o[(s - 1, 3)])
                if not (s == 0 and co < 2):
                    pq, pco = (s, co - 2) if co >= 2 else (s - 1, co + NCH - 2)
                    tensor.wait_ge(ss, s_k[(pq, pco)])
                last = None
                for c in range(NCH):
                    last = tensor.matmul(
                        sl, WkTv[:, c, co * 128:(co + 1) * 128],
                        XKv[:, c, 0:512],
                        start=(c == 0), stop=(c == NCH - 1))
                inc("t", last)
                assert C["t"] == t_k[(s, co)]
            for tt in range(4):
                psl = ps_f if tt % 2 == 0 else ps_e
                if tt == 0:
                    tensor.wait_ge(ss, s_k[(s, NCH - 1)])
                    if s >= 1:
                        tensor.wait_ge(sv, v_o[(s - 1, 2)])
                if tt == 1:
                    tensor.wait_ge(ss, s_k[(s, NCH - 1)])
                if tt >= 2:
                    tensor.wait_ge(sv, v_o[(s, tt - 2)])
                last = None
                for o in range(2):
                    for c in range(NCH):
                        last = tensor.matmul(
                            psl[:, o * 512:(o + 1) * 512],
                            kactv[:, c, tt * 128:(tt + 1) * 128],
                            WvTv[:, c, o * 512:(o + 1) * 512],
                            start=(c == 0), stop=(c == NCH - 1))
                inc("t", last)
                assert C["t"] == t_o[(s, tt)]
    assert C["t"] == tc2

    @blk.scalar
    def _(scalar):
        for s in range(NSL):
            if s >= 1:
                scalar.wait_ge(sv, v_o[(s - 1, 3)])
            for tt in range(4):
                sl = ps_a2 if tt % 2 == 0 else ps_b2
                scalar.wait_ge(st_, t_r[(s, tt)])
                inc("s", scalar.activation(Rv[:, tt, :], sl[:, :], AF.Sigmoid))
                assert C["s"] == s_r[(s, tt)]
            for co in range(NCH):
                sl = ps_e[:, 0:512] if co % 2 == 0 else ps_e[:, 512:1024]
                scalar.wait_ge(st_, t_k[(s, co)])
                if s >= 1 and co == 0:
                    scalar.wait_ge(st_, t_o[(s - 1, 3)])
                inc("s", scalar.activation(FK[:, :], sl, AF.Relu))
                inc("s", scalar.activation(kactv[:, co, :], FK[:, :], AF.Square))
                assert C["s"] == s_k[(s, co)]
    assert C["s"] == sc2

    @blk.sync
    def _(sync):
        for s in range(NSL):
            for tt in range(4):
                s2 = tt % 2
                if s == 0 and tt < 2:
                    sync.wait_ge(sv, post_p4a_v)
                else:
                    pq, ptt = (s, tt - 2) if tt >= 2 else (s - 1, tt + 2)
                    sync.wait_ge(sv, v_o[(pq, ptt)])
                r0 = s * 512 + tt * 128
                inc("x", sync.dma_start(out=F1[:, s2 * 1024:(s2 + 1) * 1024],
                                        in_=x1_sp[r0:r0 + 128, :]), 16)
                assert C["x"] == x_x1[(s, tt)]
                if tt >= 1:
                    sync.wait_ge(sv, v_o[(s, tt - 1)])
                    r1 = s * 512 + (tt - 1) * 128
                    inc("x", sync.dma_start(
                        out=y_out[r1:r1 + 128, :],
                        in_=F2[:, ((tt - 1) % 2) * 1024:((tt - 1) % 2 + 1) * 1024]), 16)
                    assert C["x"] == x_y[(s, tt - 1)]
            sync.wait_ge(sv, v_o[(s, 3)])
            r1 = s * 512 + 3 * 128
            inc("x", sync.dma_start(
                out=y_out[r1:r1 + 128, :],
                in_=F2[:, 1024:2048]), 16)
            assert C["x"] == x_y[(s, 3)]
        sync.wait_ge(sx, C["x"])
    assert C["x"] == xc2

    emit_canary([(sx, C["x"]), (sv, C["v"])])

    @blk.gpsimd
    def _(gpsimd):
        gpsimd.wait_ge(sg, C["g"])

    ps_f_ctx.__exit__(None, None, None)
    ps_e_ctx.__exit__(None, None, None)
    ps_b_ctx2.__exit__(None, None, None)
    ps_a_ctx2.__exit__(None, None, None)
    stack.close()
    return nc


_CACHE = {}


def _build_cached():
    if "nc" not in _CACHE:
        _CACHE["nc"] = build_nc()
    return _CACHE["nc"]


def kernel(L0_attn_x, L0_attn_wq, L0_attn_wk_a, L0_attn_wk_b, L0_attn_wv_a,
           L0_attn_wv_b, L0_attn_wo, L0_ffn_W_r, L0_ffn_W_k, L0_ffn_W_v,
           L0_ffn_mu_r, L0_ffn_mu_k):
    f32 = np.float32
    cT = lambda a: np.ascontiguousarray(np.asarray(a, dtype=f32).T)
    cos_t, sin_t = _rope_tables()
    shared = {
        "wqT": cT(L0_attn_wq), "wkaT": cT(L0_attn_wk_a), "wkbT": cT(L0_attn_wk_b),
        "wvaT": cT(L0_attn_wv_a), "wvbT": cT(L0_attn_wv_b), "woT": cT(L0_attn_wo),
        "wrT": cT(L0_ffn_W_r), "wkT2": cT(L0_ffn_W_k), "wvT2": cT(L0_ffn_W_v),
        "cos_in": cos_t, "sin_in": sin_t,
        "mur_in": np.ascontiguousarray(
            np.asarray(L0_ffn_mu_r, dtype=f32).reshape(NCH, 128).T),
        "muk_in": np.ascontiguousarray(
            np.asarray(L0_ffn_mu_k, dtype=f32).reshape(NCH, 128).T),
        "idb_in": np.eye(128, dtype=np.float32).astype(ml_dtypes.bfloat16),
        "idf_in": np.eye(128, dtype=np.float32),
    }
    x = np.asarray(L0_attn_x, dtype=f32)
    in_maps = []
    for b in range(B):
        m = dict(shared)
        m["x_in"] = np.ascontiguousarray(x[b])
        in_maps.append(m)
    nc = _build_cached()
    res = run_bass_kernel_spmd(nc, in_maps, core_ids=list(range(B)))
    out = np.stack([np.asarray(res.results[b]["y_out"]) for b in range(B)], axis=0)
    return out.astype(np.float32)
